# version 01 (final)
# baseline (speedup 1.0000x reference)
import numpy as np
import ml_dtypes
from contextlib import ExitStack
import concourse.bass as bass
import concourse.mybir as mybir
from concourse.bass_utils import run_bass_kernel_spmd

F32 = mybir.dt.float32
BF16 = mybir.dt.bfloat16
AF = mybir.ActivationFunctionType
ALU = mybir.AluOpType
AX = mybir.AxisListType
BFNP = ml_dtypes.bfloat16

N_CORES = 8
D = 1024
T_FULL = 4096
T_OWN = 2048

ENGS = ("pe", "act", "dve", "pool", "sp")
DQ = {"sp": "spd", "pool": "pld", "act": "acd"}
NDLANE = 4
DSEMS = tuple(f"{q}{l}" for q in ("spd", "pld", "acd") for l in range(NDLANE))
SECT = {"pe": "tensor", "act": "scalar", "dve": "vector", "pool": "gpsimd", "sp": "sync"}


class Prog:

    def __init__(self):
        self.items = {e: [] for e in ENGS}
        self.cnt = {e: 0 for e in ENGS}
        self.dcnt = {q: 0 for q in DSEMS}
        self.dissue = {q: 0 for q in ("spd", "pld", "acd")}
        self.last_w = {}
        self.readers = {}

    def _gather(self, eng, reads, writes):
        toks = []
        for k in reads:
            t = self.last_w.get(k)
            if t is not None:
                toks.append(t)
        for k in writes:
            t = self.last_w.get(k)
            if t is not None:
                toks.append(t)
            toks.extend(self.readers.get(k, ()))
        waits = {}
        for (s, v) in toks:
            if s == "pe" and eng == "pe":
                continue
            waits[s] = max(waits.get(s, 0), v)
        return sorted(waits.items())

    def _book(self, tok, reads, writes):
        for k in reads:
            self.readers.setdefault(k, []).append(tok)
        for k in writes:
            self.last_w[k] = tok
            self.readers[k] = []

    def op(self, eng, emit, reads=(), writes=(), tag=""):
        waits = self._gather(eng, reads, writes)
        self.cnt[eng] += 1
        tok = (eng, self.cnt[eng])
        self.items[eng].append((waits, emit, eng, 1, tag))
        self._book(tok, reads, writes)
        return tok

    def dma(self, eng, emit, reads=(), writes=(), tag=""):
        qb = DQ[eng]
        lane = self.dissue[qb] % NDLANE
        self.dissue[qb] += 1
        q = f"{qb}{lane}"
        waits = dict(self._gather(eng, reads, writes))
        if self.dcnt[q] > 0:
            waits[q] = max(waits.get(q, 0), self.dcnt[q])
        waits = sorted(waits.items())
        self.dcnt[q] += 16
        tok = (q, self.dcnt[q])
        self.items[eng].append((waits, emit, q, 16, tag))
        self._book(tok, reads, writes)
        return tok

    def realize(self, nc, es):
        sems = {n: es.enter_context(nc.semaphore(n)) for n in ENGS + DSEMS}
        blk = es.enter_context(nc.Block())
        final_d = dict(self.dcnt)

        def make_body(eng):
            items = self.items[eng]
            is_sp = eng == "sp"

            def body(e):
                hwm = {}
                for waits, emit, inc_sem, inc_amt, tag in items:
                    for s, v in waits:
                        if hwm.get(s, 0) >= v:
                            continue
                        e.wait_ge(sems[s], v)
                        hwm[s] = v
                    inst = emit(e)
                    if inst is not None:
                        inst.then_inc(sems[inc_sem], inc_amt)
                if is_sp:
                    for q, v in final_d.items():
                        if v > 0:
                            e.wait_ge(sems[q], v)

            return body

        for eng in ENGS:
            if self.items[eng] or eng == "sp":
                getattr(blk, SECT[eng])(make_body(eng))


def f32c(a):
    return np.ascontiguousarray(a, dtype=np.float32)


class Part:

    def __init__(self):
        self.nc = bass.Bass("TRN2", target_bir_lowering=False, debug=False)
        self.es = ExitStack()
        self.p = Prog()
        self.host_const = {}

    def din(self, name, shape, dt=F32):
        return self.nc.dram_tensor(name, list(shape), dt, kind="ExternalInput").ap()

    def dout(self, name, shape, dt=F32):
        return self.nc.dram_tensor(name, list(shape), dt, kind="ExternalOutput").ap()

    def sb(self, name, shape, dt=F32):
        return self.es.enter_context(self.nc.sbuf_tensor(name, shape, dt))

    def ps(self, name, shape, dt=F32):
        return self.es.enter_context(self.nc.psum_tensor(name, shape, dt))

    def const(self, name, arr):
        arr = np.ascontiguousarray(arr)
        dt = BF16 if arr.dtype == BFNP else F32
        self.host_const[name] = arr
        return self.din(name, arr.shape, dt)

    def load_const_sb(self, name, arr, key=None):
        ap = self.const(name, arr)
        dt = BF16 if arr.dtype == BFNP else F32
        t = self.sb(name + "_sb", list(arr.shape), dt)
        k = key or (name,)
        self.p.dma("sp", (lambda e: e.dma_start(out=t[:], in_=ap[:])), writes=(k,), tag="ld" + name)
        return t

    def finish(self):
        self.p.realize(self.nc, self.es)
        if hasattr(self, "cvals_list"):
            vals = np.zeros(16, np.float32)
            vals[:len(self.cvals_list)] = self.cvals_list
            self.host_const["cvals"] = np.tile(vals[None, :], (128, 1)).astype(np.float32)


def common_consts(P):
    P.identb = P.load_const_sb("identb", np.eye(128).astype(BFNP), key=("ident",))
    P.ones1 = P.load_const_sb("ones1", np.ones((1, 128), np.float32))
    P.cvals_list = []
    P.cvals_ap = P.din("cvals", [128, 16])
    P.cvals = P.sb("cvals_sb", [128, 16])
    P.p.dma("sp", (lambda e: e.dma_start(out=P.cvals[:], in_=P.cvals_ap[:])),
            writes=(("cvals",),), tag="ldcvals")

    def cval(v):
        v = float(v)
        if v not in P.cvals_list:
            assert len(P.cvals_list) < 16
            P.cvals_list.append(v)
        i = P.cvals_list.index(v)
        return P.cvals[:, i:i + 1]

    P.cval = cval


def emit_ln(P, x_hbm, tok0, nblk, xlnT, ntok_cols, lnb, rms_accum=None, keep_x=None):
    p = P.p
    for b in range(nblk):
        s = b % 2
        xs = lnb["x_sb"][:, s * 1024:(s + 1) * 1024]
        xk = ("lnx", s)
        p.dma("sp", (lambda e, xs=xs, b=b: e.dma_start(
            out=xs, in_=x_hbm[tok0 + b * 128: tok0 + (b + 1) * 128, :])),
            writes=(xk,), tag="ldx")
        stk = ("lnst", s)
        p.op("dve", (lambda e, xs=xs, s=s: (
            e.bn_stats(lnb["stats"][:, s * 12: s * 12 + 6], xs[:, 0:512]),
            e.bn_stats(lnb["stats"][:, s * 12 + 6: s * 12 + 12], xs[:, 512:1024]))[-1]),
            reads=(xk,), writes=(("lnst6", s),), tag="bnstats")
        p.op("dve", (lambda e, s=s: e.bn_aggr(
            lnb["mv"][:, s * 2: s * 2 + 2], lnb["stats"][:, s * 12: s * 12 + 12])),
            reads=(("lnst6", s),), writes=(stk,), tag="bnaggr")
        p.op("act", (lambda e, s=s: e.activation(
            lnb["rstd"][:, s * 2: s * 2 + 1], lnb["mv"][:, s * 2 + 1: s * 2 + 2],
            AF.Sqrt, bias=P.cval(1e-5), scale=1.0)),
            reads=(stk, ("cvals",)), writes=(("lnsq", s),), tag="lnsqrt")
        p.op("dve", (lambda e, s=s: e.reciprocal(
            lnb["rstd"][:, s * 2 + 1: s * 2 + 2], lnb["rstd"][:, s * 2: s * 2 + 1])),
            reads=(("lnsq", s),), writes=(("lnrs", s),), tag="lnrecip")
        mk = ("lntm", s)
        p.op("dve", (lambda e, xs=xs, s=s: e.tensor_scalar(
            lnb["xln_tm"][:, s * 1024:(s + 1) * 1024], xs,
            lnb["mv"][:, s * 2: s * 2 + 1], lnb["rstd"][:, s * 2 + 1: s * 2 + 2],
            ALU.subtract, ALU.mult)),
            reads=(xk, ("lnrs", s)), writes=(mk,), tag="lnnorm")
        if rms_accum is not None:
            rms_out, rms_scr, bi = rms_accum[0], rms_accum[1], b
            p.op("act", (lambda e, s=s, bi=bi: e.activation(
                rms_scr[:, 0:1024], lnb["xln_tm"][:, s * 1024:(s + 1) * 1024],
                AF.Square, accum_out=rms_out[:, bi:bi + 1])),
                reads=(mk,), writes=(("rmsscr",), ("rmsout", bi)), tag="rms")
        if keep_x is not None:
            kt, koff = keep_x
            p.op("dve", (lambda e, xs=xs, b=b: e.tensor_copy(
                kt[:, (koff + b) * 1024:(koff + b + 1) * 1024], xs)),
                reads=(xk,), writes=(("keptx", koff + b),), tag="keepx")
        for g in range(2):
            p.op("pe", (lambda e, s=s, g=g: [
                P.nc.tensor.transpose(
                    lnb["pT"][:, i * 128:(i + 1) * 128],
                    lnb["xln_tm"][:, s * 1024 + (g * 4 + i) * 128: s * 1024 + (g * 4 + i + 1) * 128],
                    P.identb[:])
                for i in range(4)][-1]),
                reads=(mk, ("ident",)), writes=(("lnpT",),), tag="lntr")
            p.op("dve", (lambda e, b=b, g=g: [
                e.tensor_copy(
                    xlnT[:, (g * 4 + i) * ntok_cols + b * 128:
                         (g * 4 + i) * ntok_cols + b * 128 + 128],
                    lnb["pT"][:, i * 128:(i + 1) * 128])
                for i in range(4)][-1]),
                reads=(("lnpT",),), writes=(("xlnT", b, g),), tag="lnevac")


def make_lnb(P):
    return {
        "x_sb": P.sb("ln_x", [128, 2 * 1024], F32),
        "stats": P.sb("ln_stats", [128, 24], F32),
        "mv": P.sb("ln_mv", [128, 4], F32),
        "rstd": P.sb("ln_rstd", [128, 4], F32),
        "xln_tm": P.sb("ln_tm", [128, 2 * 1024], BF16),
        "pT": P.ps("ln_pT", [128, 512], BF16),
    }


XLNT_KEYS = tuple(("xlnT", b, g) for b in range(4) for g in range(2))
XLNT_KEYS32 = tuple(("xlnT", b, g) for b in range(32) for g in range(2))



def build_L0f():
    P = Part()
    p = P.p
    xin = P.din("xin", [T_OWN, D])
    w1T = P.din("w1T", [D, 4096])
    w2T = P.din("w2T", [D, 4096])
    h1T = P.din("h1T", [4096, 4096])
    h2T = P.din("h2T", [4096, 4096])
    w3T = P.din("w3T", [4096, D])
    s1d = P.din("s1", [128, 32])
    s2d = P.din("s2", [128, 32])
    sh1d = P.din("sh1", [128, 32])
    sh2d = P.din("sh2", [128, 32])
    s3row = P.din("s3row", [1, D])
    xout = P.dout("xout", [T_OWN, D])

    common_consts(P)
    lnb = make_lnb(P)
    xlnT = P.sb("xlnT", [128, 8 * 512], BF16)
    a_sb = P.sb("a_sb", [128, 32 * 512], BF16)
    g_sb = P.sb("g_sb", [128, 32 * 512], BF16)
    sil = P.sb("sil", [128, 512], BF16)
    wA = [P.sb(f"wA{i}", [128, 8 * 128], BF16) for i in range(2)]
    wB = [P.sb(f"wB{i}", [128, 8 * 128], BF16) for i in range(2)]
    wH = [P.sb(f"wH{i}", [128, 32 * 128], BF16) for i in range(2)]
    w3sb = P.sb("w3sb", [128, 32 * 512], BF16)
    s1 = P.sb("s1s", [128, 32])
    s2 = P.sb("s2s", [128, 32])
    sh1 = P.sb("sh1s", [128, 32])
    sh2 = P.sb("sh2s", [128, 32])
    s3r = P.sb("s3r", [1, D])
    s3B = P.sb("s3B", [128, D])
    xepi = P.sb("xepi", [128, 2 * 1024], F32)
    onew = P.sb("onew", [128, 2 * 1024], F32)
    pA = P.ps("pA", [128, 512])
    pB = P.ps("pB", [128, 512])
    pH = [P.ps(f"pH{i}", [128, 512]) for i in range(2)]
    pO = [P.ps(f"pO{i}", [128, 512]) for i in range(2)]
    psBc = P.ps("psBc", [128, 512])

    for nm, sbuf, dram in (("s1", s1, s1d), ("s2", s2, s2d), ("sh1", sh1, sh1d),
                           ("sh2", sh2, sh2d), ("s3r", s3r, s3row)):
        p.dma("sp", (lambda e, sbuf=sbuf, dram=dram: e.dma_start(out=sbuf[:], in_=dram[:])),
              writes=((nm,),), tag="ld" + nm)
    p.op("pe", (lambda e: [e.matmul(psBc[:, i * 512:(i + 1) * 512] if False else psBc[:],
                                    P.ones1[0:1, :], s3r[0:1, i * 512:(i + 1) * 512],
                                    start=True, stop=True) for i in range(1)][-1]),
         reads=(("s3r",), ("ones1",)), writes=(("psBc",),), tag="s3bc0")
    p.op("dve", (lambda e: e.tensor_copy(s3B[:, 0:512], psBc[:])),
         reads=(("psBc",),), writes=(("s3B", 0),), tag="s3ev0")
    p.op("pe", (lambda e: e.matmul(psBc[:], P.ones1[0:1, :], s3r[0:1, 512:1024],
                                   start=True, stop=True)),
         reads=(("s3r",), ("ones1",), ("s3B", 0)), writes=(("psBc",),), tag="s3bc1")
    p.op("dve", (lambda e: e.tensor_copy(s3B[:, 512:1024], psBc[:])),
         reads=(("psBc",),), writes=(("s3B", 1),), tag="s3ev1")

    A_KEYS = tuple(("a", m) for m in range(32))
    G_KEYS = tuple(("g", m) for m in range(32))

    for su in range(4):
        tok0 = su * 512
        emit_ln(P, xin, tok0, 4, xlnT, 512, lnb)
        for m in range(32):
            wbA, wbB = wA[m % 2], wB[m % 2]
            kA, kB = ("wA", m % 2), ("wB", m % 2)
            p.dma("pool", (lambda e, wbA=wbA, m=m: e.dma_start(
                out=wbA[:].rearrange("p (c n) -> p c n", n=128), in_=w1T[:, m * 128:(m + 1) * 128].rearrange("(c p) n -> p c n", p=128))),
                writes=(kA,), tag="w1")
            p.dma("pool", (lambda e, wbB=wbB, m=m: e.dma_start(
                out=wbB[:].rearrange("p (c n) -> p c n", n=128), in_=w2T[:, m * 128:(m + 1) * 128].rearrange("(c p) n -> p c n", p=128))),
                writes=(kB,), tag="w2")
            def mma(e, wbA=wbA):
                r = None
                for c in range(8):
                    r = e.matmul(pA[:], wbA[:, c * 128:(c + 1) * 128],
                                 xlnT[:, c * 512:(c + 1) * 512], start=(c == 0), stop=(c == 7))
                return r
            def mmb(e, wbB=wbB):
                r = None
                for c in range(8):
                    r = e.matmul(pB[:], wbB[:, c * 128:(c + 1) * 128],
                                 xlnT[:, c * 512:(c + 1) * 512], start=(c == 0), stop=(c == 7))
                return r
            p.op("pe", mma, reads=(kA,) + XLNT_KEYS, writes=(("pA",),), tag="mmA")
            p.op("pe", mmb, reads=(kB,) + XLNT_KEYS, writes=(("pB",),), tag="mmB")
            p.op("act", (lambda e, m=m: e.activation(sil[:], pA[:], AF.Silu, scale=s1[:, m:m + 1])),
                 reads=(("pA",), ("s1",)), writes=(("sil",),), tag="silu")
            p.op("dve", (lambda e, m=m: e.scalar_tensor_tensor(
                a_sb[:, m * 512:(m + 1) * 512], pB[:], s2[:, m:m + 1], sil[:],
                ALU.mult, ALU.mult)),
                reads=(("pB",), ("sil",), ("s2",)), writes=(("a", m),), tag="glu")
        for m in range(32):
            wb = wH[m % 2]
            kw = ("wH", m % 2)
            p.dma("pool", (lambda e, wb=wb, m=m: e.dma_start(
                out=wb[:].rearrange("p (c n) -> p c n", n=128), in_=h1T[:, m * 128:(m + 1) * 128].rearrange("(c p) n -> p c n", p=128))),
                writes=(kw,), tag="h1w")
            ph = pH[m % 2]
            psk = ("pH", m % 2)
            def mmh(e, wb=wb, ph=ph):
                r = None
                for c in range(32):
                    r = e.matmul(ph[:], wb[:, c * 128:(c + 1) * 128],
                                 a_sb[:, c * 512:(c + 1) * 512], start=(c == 0), stop=(c == 31))
                return r
            p.op("pe", mmh, reads=(kw,) + A_KEYS, writes=(psk,), tag="mmH1")
            p.op("act", (lambda e, m=m, ph=ph: e.activation(
                g_sb[:, m * 512:(m + 1) * 512], ph[:], AF.Gelu, scale=sh1[:, m:m + 1])),
                reads=(psk, ("sh1",)), writes=(("g", m),), tag="gelu1")
        for m in range(32):
            wb = wH[m % 2]
            kw = ("wH", m % 2)
            p.dma("pool", (lambda e, wb=wb, m=m: e.dma_start(
                out=wb[:].rearrange("p (c n) -> p c n", n=128), in_=h2T[:, m * 128:(m + 1) * 128].rearrange("(c p) n -> p c n", p=128))),
                writes=(kw,), tag="h2w")
            ph = pH[m % 2]
            psk = ("pH", m % 2)
            def mmh2(e, wb=wb, ph=ph):
                r = None
                for c in range(32):
                    r = e.matmul(ph[:], wb[:, c * 128:(c + 1) * 128],
                                 g_sb[:, c * 512:(c + 1) * 512], start=(c == 0), stop=(c == 31))
                return r
            p.op("pe", mmh2, reads=(kw,) + G_KEYS, writes=(psk,), tag="mmH2")
            p.op("act", (lambda e, m=m, ph=ph: e.activation(
                a_sb[:, m * 512:(m + 1) * 512], ph[:], AF.Gelu, scale=sh2[:, m:m + 1])),
                reads=(psk, ("sh2",)), writes=(("a", m),), tag="gelu2")
        for half in range(2):
            p.dma("pool", (lambda e, half=half: e.dma_start(
                out=w3sb[:].rearrange("p (c n) -> p c n", n=512),
                in_=w3T[:, half * 512:(half + 1) * 512].rearrange("(c p) n -> p c n", p=128))),
                writes=(("w3",),), tag="w3w")
            for tb in range(4):
                po = pO[tb % 2]
                psk = ("pO", tb % 2)
                def mmo(e, po=po, tb=tb):
                    r = None
                    for c in range(32):
                        r = e.matmul(po[:], a_sb[:, c * 512 + tb * 128: c * 512 + tb * 128 + 128],
                                     w3sb[:, c * 512:(c + 1) * 512], start=(c == 0), stop=(c == 31))
                    return r
                p.op("pe", mmo, reads=(("w3",),) + A_KEYS, writes=(psk,), tag="mmW3")
                ek = ("xepi", tb % 2)
                xe = xepi[:, (tb % 2) * 1024:(tb % 2 + 1) * 1024]
                p.dma("sp", (lambda e, xe=xe, tb=tb, tok0=tok0: e.dma_start(
                    out=xe, in_=xin[tok0 + tb * 128: tok0 + (tb + 1) * 128, :])),
                    writes=(ek,), tag="ldxe")
                ow = onew[:, (tb % 2) * 1024 + half * 512:(tb % 2) * 1024 + (half + 1) * 512]
                owk = ("onew", tb % 2, half)
                p.op("dve", (lambda e, po=po, ow=ow, half=half: e.tensor_mul(
                    ow, po[:], s3B[:, half * 512:(half + 1) * 512])),
                    reads=(psk, ("s3B", half)), writes=(owk,), tag="epi1")
                p.op("dve", (lambda e, ow=ow, xe=xe, half=half: e.tensor_add(
                    ow, ow, xe[:, half * 512:(half + 1) * 512])),
                    reads=(ek, owk), writes=(owk,), tag="epi2")
                p.dma("sp", (lambda e, ow=ow, tb=tb, tok0=tok0, half=half: e.dma_start(
                    out=xout[tok0 + tb * 128: tok0 + (tb + 1) * 128, half * 512:(half + 1) * 512],
                    in_=ow)),
                    reads=(owk,), tag="stx")

    P.finish()
    return P


def stage_L0f_inputs(inputs, core):
    b, h = core // 2, core % 2
    x = inputs["_x_cur"]
    return {
        "xin": f32c(x[b, h * T_OWN:(h + 1) * T_OWN, :]),
        "w1T": f32c(inputs["L0_ffn_w1_w"].T),
        "w2T": f32c(inputs["L0_ffn_w2_w"].T),
        "h1T": f32c(inputs["L0_ffn_h1_w"].T),
        "h2T": f32c(inputs["L0_ffn_h2_w"].T),
        "w3T": f32c(inputs["L0_ffn_w3_w"].T),
        "s1": f32c(np.asarray(inputs["L0_ffn_w1_s"]).reshape(32, 128).T),
        "s2": f32c(np.asarray(inputs["L0_ffn_w2_s"]).reshape(32, 128).T),
        "sh1": f32c(np.asarray(inputs["L0_ffn_h1_s"]).reshape(32, 128).T),
        "sh2": f32c(np.asarray(inputs["L0_ffn_h2_s"]).reshape(32, 128).T),
        "s3row": f32c(np.asarray(inputs["L0_ffn_w3_s"]).reshape(1, D)),
    }




DBG_L1A = False


def build_L1a():
    P = Part()
    p = P.p
    xin = P.din("xin", [T_FULL, D])
    wqT = P.din("wqT", [D, D])
    wkT = P.din("wkT", [D, D])
    wvT = P.din("wvT", [D, D])
    woT = P.din("woT", [D, D])
    bq128 = P.din("bq128", [128, 8])
    bk128 = P.din("bk128", [128, 8])
    bvrow = P.din("bvrow", [1, D])
    borow = P.din("borow", [1, D])
    alph = P.din("alph", [128, 1])
    lag = P.din("lag", [T_FULL, T_OWN])
    xout = P.dout("xout", [T_OWN, D])

    common_consts(P)
    lnb = make_lnb(P)
    onescol = P.load_const_sb("onescol", np.ones((128, 1), np.float32))
    onescb = P.load_const_sb("onescb", np.ones((128, 1)).astype(BFNP))
    xlnT = P.sb("xlnT", [128, 8 * T_FULL], BF16)
    kT = P.sb("kT", [128, 2 * T_FULL], BF16)
    qT = P.sb("qT", [128, 2 * T_OWN], BF16)
    v_sb = P.sb("v_sb", [128, 32 * 260], BF16)
    yT = P.sb("yT", [128, 8 * T_OWN], BF16)
    w_hg = P.sb("w_hg", [128, 8 * 256], BF16)
    wo_sb = P.sb("wo_sb", [128, 8 * 512], BF16)
    bq_sb = P.sb("bq_sb", [128, 8])
    bk_sb = P.sb("bk_sb", [128, 8])
    bv_r = P.sb("bv_r", [1, D])
    bo_r = P.sb("bo_r", [1, D])
    al_sb = P.sb("al_sb", [128, 1])
    nal_sb = P.sb("nal_sb", [128, 1])
    lag_sb = P.sb("lag_sb", [128, 2 * 512])
    dec_sb = P.sb("dec_sb", [128, 512])
    tsc = P.sb("tsc", [128, 2 * 512])
    p_sb = P.sb("p_sb", [128, 2 * 512], BF16)
    sq_sb = P.sb("sq_sb", [128, 512], BF16)
    sm_sb = P.sb("sm_sb", [1, 16])
    nb0_sb = P.sb("nb0_sb", [128, 1])
    rdrow = P.sb("rdrow", [1, 512])
    rdB = P.sb("rdB", [64, 512])
    xepi = P.sb("xepi", [128, 2 * 1024])
    onew = P.sb("onew", [128, 2 * 1024])

    if DBG_L1A:
        dbg_y = P.dout("dbg_y", [128, 8 * T_OWN])
        dbg_dec = P.dout("dbg_dec", [128, 512])
        dbg_nb0 = P.dout("dbg_nb0", [128, 4])
        nb0_hist = P.sb("nb0_hist", [128, 4])
    acc = [P.ps(f"acc{j}", [65, 512]) for j in range(4)]
    pw = [P.ps(f"pw{i}", [128, 512]) for i in range(2)]
    pB = P.ps("pB", [128, 512])

    for nm, sbuf, dram in (("bq", bq_sb, bq128), ("bk", bk_sb, bk128), ("bv", bv_r, bvrow),
                           ("bo", bo_r, borow), ("al", al_sb, alph)):
        p.dma("sp", (lambda e, sbuf=sbuf, dram=dram: e.dma_start(out=sbuf[:], in_=dram[:])),
              writes=((nm,),), tag="ld" + nm)
    p.op("dve", (lambda e: e.tensor_scalar(nal_sb[:], al_sb[:], P.cval(-1.0), None, ALU.mult)),
         reads=(("al",), ("cvals",)), writes=(("nal",),), tag="negalpha")
    p.op("act", (lambda e: e.activation(sm_sb[0:1, 2:3], al_sb[0:1, 0:1], AF.Exp, scale=-4095.0)),
         reads=(("al",),), writes=(("smd0",),), tag="maxd0")
    p.op("dve", (lambda e: e.tensor_scalar(sm_sb[0:1, 3:4], sm_sb[0:1, 2:3],
                                           P.cval(1.0)[0:1, :], None, ALU.max)),
         reads=(("smd0",), ("cvals",)), writes=(("smd",),), tag="maxd")
    p.op("dve", (lambda e: e.memset(v_sb[:], 1.0)), writes=(("vones",),), tag="vones")

    emit_ln(P, xin, 0, 32, xlnT, T_FULL, lnb)

    HD = 64
    C_ATT = 0.125

    for hg in range(4):
        p.dma("pool", (lambda e, hg=hg: e.dma_start(
            out=w_hg[:].rearrange("p (c n) -> p c n", n=256),
            in_=wkT[:, hg * 256:(hg + 1) * 256].rearrange("(c p) n -> p c n", p=128))),
            writes=(("w_hg",),), tag="wk")
        for m2 in range(2):
            for kt in range(8):
                def mmk(e, m2=m2, kt=kt):
                    r = None
                    for c in range(8):
                        r = e.matmul(pw[kt % 2][:], w_hg[:, c * 256 + m2 * 128: c * 256 + (m2 + 1) * 128],
                                     xlnT[:, c * T_FULL + kt * 512: c * T_FULL + kt * 512 + 512],
                                     start=(c == 0), stop=(c == 7))
                    return r
                p.op("pe", mmk, reads=(("w_hg",),) + XLNT_KEYS, writes=(("pw", kt % 2),), tag="mmk")
                p.op("act", (lambda e, m2=m2, kt=kt, hg=hg: e.activation(
                    kT[:, m2 * T_FULL + kt * 512: m2 * T_FULL + kt * 512 + 512],
                    pw[kt % 2][:], AF.Identity, bias=bk_sb[:, hg * 2 + m2: hg * 2 + m2 + 1])),
                    reads=(("pw", kt % 2), ("bk",)), writes=(("kT", m2, kt),), tag="kevac")
        p.dma("pool", (lambda e, hg=hg: e.dma_start(
            out=w_hg[:].rearrange("p (c n) -> p c n", n=256),
            in_=wqT[:, hg * 256:(hg + 1) * 256].rearrange("(c p) n -> p c n", p=128))),
            reads=tuple(("kT", m2, kt) for m2 in range(2) for kt in range(8)),
            writes=(("w_hg",),), tag="wq")
        for m2 in range(2):
            for qt2 in range(4):
                def mmq(e, m2=m2, qt2=qt2):
                    r = None
                    for c in range(8):
                        r = e.matmul(pw[qt2 % 2][:], w_hg[:, c * 256 + m2 * 128: c * 256 + (m2 + 1) * 128],
                                     xlnT[:, c * T_FULL + qt2 * 512: c * T_FULL + qt2 * 512 + 512],
                                     start=(c == 0), stop=(c == 7))
                    return r
                p.op("pe", mmq, reads=(("w_hg",),) + XLNT_KEYS, writes=(("pw", qt2 % 2),), tag="mmq")
                p.op("act", (lambda e, m2=m2, qt2=qt2, hg=hg: e.activation(
                    qT[:, m2 * T_OWN + qt2 * 512: m2 * T_OWN + qt2 * 512 + 512],
                    pw[qt2 % 2][:], AF.Identity, bias=bq_sb[:, hg * 2 + m2: hg * 2 + m2 + 1])),
                    reads=(("pw", qt2 % 2), ("bq",)), writes=(("qT", m2, qt2),), tag="qevac")
        p.dma("pool", (lambda e, hg=hg: e.dma_start(
            out=w_hg[:].rearrange("p (c n) -> p c n", n=256),
            in_=wvT[:, hg * 256:(hg + 1) * 256].rearrange("(c p) n -> p c n", p=128))),
            reads=tuple(("qT", m2, qt2) for m2 in range(2) for qt2 in range(4)),
            writes=(("w_hg",),), tag="wv")
        for kb in range(32):
            def mmv(e, kb=kb, hg=hg):
                r = None
                for c in range(8):
                    r = e.matmul(pw[kb % 2][:, 0:256],
                                 xlnT[:, c * T_FULL + kb * 128: c * T_FULL + kb * 128 + 128],
                                 w_hg[:, c * 256:(c + 1) * 256], start=(c == 0), stop=False)
                r = e.matmul(pw[kb % 2][:, 0:256], P.ones1[0:1, :],
                             bv_r[0:1, hg * 256:(hg + 1) * 256], start=False, stop=True)
                return r
            p.op("pe", mmv, reads=(("w_hg",), ("bv",), ("ones1",)) + XLNT_KEYS,
                 writes=(("pw", kb % 2),), tag="mmv")
            p.op("act", (lambda e, kb=kb: e.activation(
                v_sb[:, kb * 260:(kb + 1) * 260].rearrange("p (j n) -> p j n", j=4)[:, :, 0:64],
                pw[kb % 2][:, 0:256].rearrange("p (j n) -> p j n", j=4), AF.Copy)),
                reads=(("pw", kb % 2), ("vones",)), writes=(("v", kb),), tag="vevac")
        p.op("dve", (lambda e: e.memset(sm_sb[0:1, 0:2], 0.0)), writes=(("smqk",),), tag="b0z")
        for which, nslices, toks, src_cols in (("q", 4, T_OWN, 2), ("k", 8, T_FULL, 2)):
            for m2 in range(2):
                for sl in range(nslices):
                    sqk = ("sq",)
                    src = qT if which == "q" else kT
                    srck = ("qT", m2, sl) if which == "q" else ("kT", m2, sl)
                    p.op("act", (lambda e, src=src, m2=m2, sl=sl, toks=toks: e.activation(
                        sq_sb[:], src[:, m2 * toks + sl * 512: m2 * toks + sl * 512 + 512],
                        AF.Square)),
                        reads=(srck,), writes=(sqk,), tag="b0sq")
                    for jh in range(2):
                        def mmn(e, jh=jh):
                            return e.matmul(pB[0:1, 0:512], onescb[jh * 64:(jh + 1) * 64, 0:1],
                                            sq_sb[jh * 64:(jh + 1) * 64, :], start=True, stop=True)
                        p.op("pe", mmn, reads=(sqk, ("onescb",)), writes=(("pBn",),), tag="b0mm")
                        p.op("dve", (lambda e: e.reduce_max(sm_sb[0:1, 4:5], pB[0:1, 0:512], axis=AX.X)),
                             reads=(("pBn",),), writes=(("smt",),), tag="b0red")
                        widx = 0 if which == "q" else 1
                        p.op("dve", (lambda e, widx=widx: e.tensor_max(
                            sm_sb[0:1, widx:widx + 1], sm_sb[0:1, widx:widx + 1], sm_sb[0:1, 4:5])),
                            reads=(("smt",), ("smqk",)), writes=(("smqk",),), tag="b0max")
        p.op("dve", (lambda e: e.tensor_mul(sm_sb[0:1, 5:6], sm_sb[0:1, 0:1], sm_sb[0:1, 1:2])),
             reads=(("smqk",),), writes=(("smp",),), tag="b0mul")
        p.op("act", (lambda e: e.activation(sm_sb[0:1, 6:7], sm_sb[0:1, 5:6], AF.Sqrt)),
             reads=(("smp",),), writes=(("smsq",),), tag="b0sqrt")
        p.op("dve", (lambda e: e.scalar_tensor_tensor(
            sm_sb[0:1, 7:8], sm_sb[0:1, 6:7], P.cval(-C_ATT)[0:1, :], sm_sb[0:1, 3:4],
            ALU.mult, ALU.mult)),
            reads=(("smsq",), ("smd",), ("cvals",)), writes=(("smb0",),), tag="b0fin")
        p.op("pe", (lambda e: e.matmul(pB[:, 0:1], P.ones1[0:1, :], sm_sb[0:1, 7:8],
                                       start=True, stop=True)),
             reads=(("smb0",), ("ones1",)), writes=(("pBn",),), tag="b0bc")
        p.op("dve", (lambda e: e.tensor_copy(nb0_sb[:], pB[:, 0:1])),
             reads=(("pBn",),), writes=(("nb0",),), tag="b0ev")
        if DBG_L1A:
            p.op("dve", (lambda e, hg=hg: e.tensor_copy(nb0_hist[:, hg:hg + 1], nb0_sb[:])),
                 reads=(("nb0",),), writes=(("nb0h",),), tag="dbg")
        KT_KEYS = tuple(("kT", m2, kt) for m2 in range(2) for kt in range(8))
        QT_KEYS = tuple(("qT", m2, qt2) for m2 in range(2) for qt2 in range(4))
        V_KEYS = tuple(("v", kb) for kb in range(32))
        for qt in range(4):
            for kc in range(32):
                ls = kc % 2
                p.dma("sp", (lambda e, qt=qt, kc=kc, ls=ls: e.dma_start(
                    out=lag_sb[:, ls * 512:(ls + 1) * 512],
                    in_=lag[kc * 128:(kc + 1) * 128, qt * 512:(qt + 1) * 512])),
                    writes=(("lag", ls),), tag="lagld")
                p.op("act", (lambda e, ls=ls: e.activation(
                    dec_sb[:], lag_sb[:, ls * 512:(ls + 1) * 512], AF.Exp, scale=nal_sb[:])),
                    reads=(("lag", ls), ("nal",)), writes=(("dec",),), tag="decay")
                for j in range(4):
                    m2, rb = j // 2, (j % 2) * 64
                    ps_s = pw[j % 2]
                    psk = ("pw", j % 2)
                    def mms(e, m2=m2, rb=rb, ps_s=ps_s, kc=kc, qt=qt):
                        return e.matmul(ps_s[:],
                                        kT[rb:rb + 64, m2 * T_FULL + kc * 128: m2 * T_FULL + kc * 128 + 128],
                                        qT[rb:rb + 64, m2 * T_OWN + qt * 512: m2 * T_OWN + qt * 512 + 512],
                                        start=True, stop=True)
                    p.op("pe", mms, reads=KT_KEYS + QT_KEYS, writes=(psk,), tag="mms")
                    tslot = j % 2
                    p.op("dve", (lambda e, ps_s=ps_s, tslot=tslot: e.scalar_tensor_tensor(
                        tsc[:, tslot * 512:(tslot + 1) * 512], ps_s[:], P.cval(C_ATT), dec_sb[:],
                        ALU.mult, ALU.mult)),
                        reads=(psk, ("dec",), ("cvals",)), writes=(("tsc", tslot),), tag="pdec")
                    p.op("act", (lambda e, tslot=tslot: e.activation(
                        p_sb[:, tslot * 512:(tslot + 1) * 512], tsc[:, tslot * 512:(tslot + 1) * 512],
                        AF.Exp, bias=nb0_sb[:])),
                        reads=(("tsc", tslot), ("nb0",)), writes=(("p", tslot),), tag="pexp")
                    def mmpv(e, j=j, kc=kc, tslot=tslot):
                        return e.matmul(acc[j][:], v_sb[:, kc * 260 + j * 65: kc * 260 + j * 65 + 65],
                                        p_sb[:, tslot * 512:(tslot + 1) * 512],
                                        start=(kc == 0), stop=(kc == 31))
                    p.op("pe", mmpv, reads=(("p", tslot),) + V_KEYS, writes=(("acc", j),), tag="mmpv")
            for j in range(4):
                jg = hg * 4 + j
                p.op("dve", (lambda e, j=j: e.reciprocal(rdrow[:], acc[j][64:65, :])),
                     reads=(("acc", j),), writes=(("rdrow",),), tag="recip")
                p.op("pe", (lambda e: e.matmul(pB[0:64, :], P.ones1[0:1, 0:64], rdrow[0:1, :],
                                               start=True, stop=True)),
                     reads=(("rdrow",), ("ones1",)), writes=(("pBn",),), tag="rdbc")
                p.op("act", (lambda e: e.activation(rdB[:], pB[0:64, :], AF.Copy)),
                     reads=(("pBn",),), writes=(("rdB",),), tag="rdev")
                p.op("dve", (lambda e, j=j, jg=jg, qt=qt: e.tensor_mul(
                    yT[(jg % 2) * 64:(jg % 2) * 64 + 64,
                       (jg // 2) * T_OWN + qt * 512:(jg // 2) * T_OWN + qt * 512 + 512],
                    acc[j][0:64, :], rdB[:])),
                    reads=(("acc", j), ("rdB",)), writes=(("yT", jg, qt),), tag="ydiv")
    if DBG_L1A:
        p.dma("pool", (lambda e: e.dma_start(out=dbg_y[:], in_=yT[:])),
              reads=tuple(("yT", jg, qt) for jg in range(16) for qt in range(4)), tag="dbg")
        p.dma("pool", (lambda e: e.dma_start(out=dbg_dec[:], in_=dec_sb[:])),
              reads=(("dec",),), tag="dbg")
        p.dma("sp", (lambda e: e.dma_start(out=dbg_nb0[:], in_=nb0_hist[:])),
              reads=(("nb0h",),), tag="dbg")
    YT_KEYS = tuple(("yT", jg, qt) for jg in range(16) for qt in range(4))
    for dhalf in range(2):
        p.dma("pool", (lambda e, dhalf=dhalf: e.dma_start(
            out=wo_sb[:].rearrange("p (c n) -> p c n", n=512),
            in_=woT[:, dhalf * 512:(dhalf + 1) * 512].rearrange("(c p) n -> p c n", p=128))),
            writes=(("wo",),), tag="wo")
        for tb in range(16):
            po = pw[tb % 2]
            psk = ("pw", tb % 2)
            def mmo(e, po=po, tb=tb, dhalf=dhalf):
                r = None
                for c in range(8):
                    r = e.matmul(po[:], yT[:, c * T_OWN + tb * 128: c * T_OWN + tb * 128 + 128],
                                 wo_sb[:, c * 512:(c + 1) * 512], start=(c == 0), stop=False)
                r = e.matmul(po[:], P.ones1[0:1, :], bo_r[0:1, dhalf * 512:(dhalf + 1) * 512],
                             start=False, stop=True)
                return r
            p.op("pe", mmo, reads=(("wo",), ("bo",), ("ones1",)) + YT_KEYS, writes=(psk,), tag="mmo")
            ek = ("xepi", tb % 2)
            xe = xepi[:, (tb % 2) * 1024:(tb % 2 + 1) * 1024]
            p.dma("sp", (lambda e, xe=xe, tb=tb: e.dma_start(
                out=xe, in_=xin[tb * 128:(tb + 1) * 128, :])),
                writes=(ek,), tag="ldxe")
            ow = onew[:, (tb % 2) * 1024 + dhalf * 512:(tb % 2) * 1024 + (dhalf + 1) * 512]
            owk = ("onew", tb % 2, dhalf)
            p.op("dve", (lambda e, po=po, ow=ow, xe=xe, dhalf=dhalf: e.tensor_add(
                ow, po[:], xe[:, dhalf * 512:(dhalf + 1) * 512])),
                reads=(psk, ek), writes=(owk,), tag="oepi")
            p.dma("sp", (lambda e, ow=ow, tb=tb, dhalf=dhalf: e.dma_start(
                out=xout[tb * 128:(tb + 1) * 128, dhalf * 512:(dhalf + 1) * 512], in_=ow)),
                reads=(owk,), tag="stx")

    P.finish()
    return P


def stage_L1a_inputs(inputs, core):
    b, h = core // 2, core % 2
    x = inputs["_x_cur"][b]
    own = x[h * T_OWN:(h + 1) * T_OWN]
    other = x[(1 - h) * T_OWN:(2 - h) * T_OWN]
    xperm = np.concatenate([own, other], axis=0)
    kpos = np.concatenate([np.arange(h * T_OWN, (h + 1) * T_OWN),
                           np.arange((1 - h) * T_OWN, (2 - h) * T_OWN)]).astype(np.float32)
    qpos = kpos[:T_OWN]
    lag = np.abs(kpos[:, None] - qpos[None, :]).astype(np.float32)
    return {
        "xin": f32c(xperm),
        "wqT": f32c(inputs["L1_attn_Wq"].T),
        "wkT": f32c(inputs["L1_attn_Wk"].T),
        "wvT": f32c(inputs["L1_attn_Wv"].T),
        "woT": f32c(inputs["L1_attn_Wo"].T),
        "bq128": f32c(np.asarray(inputs["L1_attn_bq"]).reshape(8, 128).T),
        "bk128": f32c(np.asarray(inputs["L1_attn_bk"]).reshape(8, 128).T),
        "bvrow": f32c(np.asarray(inputs["L1_attn_bv"]).reshape(1, D)),
        "borow": f32c(np.asarray(inputs["L1_attn_bo"]).reshape(1, D)),
        "alph": f32c(np.broadcast_to(np.asarray(inputs["L1_attn_alpha"]).reshape(1, 1), (128, 1))),
        "lag": f32c(lag),
    }




def build_L3a():
    P = Part()
    p = P.p
    xin = P.din("xin", [T_FULL, D])
    wqT = P.din("wqT", [D, D])
    wkT = P.din("wkT", [D, D])
    wvT = P.din("wvT", [D, D])
    woT = P.din("woT", [D, D])
    bq128 = P.din("bq128", [128, 8])
    bk128 = P.din("bk128", [128, 8])
    bvrow = P.din("bvrow", [1, D])
    borow = P.din("borow", [1, D])
    ctab = P.din("ctab", [128, T_FULL], BF16)
    stab = P.din("stab", [128, T_FULL], BF16)
    xout = P.dout("xout", [T_OWN, D])

    common_consts(P)
    lnb = make_lnb(P)
    onescb = P.load_const_sb("onescb", np.ones((128, 1)).astype(BFNP))
    xlnT = P.sb("xlnT", [128, 8 * T_FULL], BF16)
    kT = P.sb("kT", [128, 2 * T_FULL], BF16)
    qT = P.sb("qT", [128, 2 * T_OWN], BF16)
    v_sb = P.sb("v_sb", [128, 32 * 260], BF16)
    yT = P.sb("yT", [128, 8 * T_OWN], BF16)
    w_hg = P.sb("w_hg", [128, 8 * 256], BF16)
    wo_sb = P.sb("wo_sb", [128, 8 * 512], BF16)
    ct_sb = P.sb("ct_sb", [128, T_FULL], BF16)
    st_sb = P.sb("st_sb", [128, T_FULL], BF16)
    rs1 = P.sb("rs1", [128, 1024], BF16)
    rs2 = P.sb("rs2", [128, 1024], BF16)
    bq_sb = P.sb("bq_sb", [128, 8])
    bk_sb = P.sb("bk_sb", [128, 8])
    bv_r = P.sb("bv_r", [1, D])
    bo_r = P.sb("bo_r", [1, D])
    p_sb = P.sb("p_sb", [128, 2 * 512], BF16)
    sq_sb = P.sb("sq_sb", [128, 512], BF16)
    sm_sb = P.sb("sm_sb", [1, 16])
    nb0_sb = P.sb("nb0_sb", [128, 1])
    rdrow = P.sb("rdrow", [1, 512])
    rdB = P.sb("rdB", [64, 512])
    xepi = P.sb("xepi", [128, 1024])

    acc = [P.ps(f"acc{j}", [65, 512]) for j in range(4)]
    pw = [P.ps(f"pw{i}", [128, 512]) for i in range(2)]
    pB = P.ps("pB", [128, 512])

    for nm, sbuf, dram in (("bq", bq_sb, bq128), ("bk", bk_sb, bk128), ("bv", bv_r, bvrow),
                           ("bo", bo_r, borow), ("ct", ct_sb, ctab), ("st", st_sb, stab)):
        p.dma("sp", (lambda e, sbuf=sbuf, dram=dram: e.dma_start(out=sbuf[:], in_=dram[:])),
              writes=((nm,),), tag="ld" + nm)
    p.op("dve", (lambda e: e.memset(v_sb[:], 1.0)), writes=(("vones",),), tag="vones")

    emit_ln(P, xin, 0, 32, xlnT, T_FULL, lnb)

    C_ATT = 0.125

    def emit_rope(dst, m2, ncols, col0, key_in, key_out):
        stride = dst.shape[1] // 2
        for half in range(ncols // 1024):
            c0 = col0 + half * 1024
            for r0 in (0, 64):
                x1 = slice(r0, r0 + 32)
                x2 = slice(r0 + 32, r0 + 64)
                dd = lambda sl, m2=m2, c0=c0: dst[sl, m2 * stride + c0: m2 * stride + c0 + 1024]
                tt = lambda sl, t, c0=c0: t[sl, c0: c0 + 1024]
                p.op("dve", (lambda e, x1=x1, dd=dd, tt=tt: e.tensor_mul(
                    rs1[x1.start:x1.start + 32, 0:1024], dd(x1), tt(x1, ct_sb))),
                    reads=(key_in, key_out, ("ct",)), writes=(("rs1", r0),), tag="rp1")
                p.op("dve", (lambda e, x1=x1, x2=x2, dd=dd, tt=tt: e.tensor_mul(
                    rs2[x1.start:x1.start + 32, 0:1024], dd(x2), tt(x2, st_sb))),
                    reads=(key_in, key_out, ("st",)), writes=(("rs2", r0),), tag="rp2")
                p.op("dve", (lambda e, x1=x1, x2=x2, dd=dd, tt=tt: e.tensor_mul(
                    rs1[x2.start:x2.start + 32, 0:1024], dd(x1), tt(x1, st_sb))),
                    reads=(key_in, key_out, ("st",)), writes=(("rs1b", r0),), tag="rp3")
                p.op("dve", (lambda e, x2=x2, dd=dd, tt=tt: e.tensor_mul(
                    rs2[x2.start:x2.start + 32, 0:1024], dd(x2), tt(x2, ct_sb))),
                    reads=(key_in, key_out, ("ct",)), writes=(("rs2b", r0),), tag="rp4")
                p.op("dve", (lambda e, x1=x1, dd=dd: e.tensor_sub(
                    dd(x1), rs1[x1.start:x1.start + 32, 0:1024],
                    rs2[x1.start:x1.start + 32, 0:1024])),
                    reads=(("rs1", r0), ("rs2", r0)), writes=(key_out,), tag="rp5")
                p.op("dve", (lambda e, x2=x2, dd=dd: e.tensor_add(
                    dd(x2), rs1[x2.start:x2.start + 32, 0:1024],
                    rs2[x2.start:x2.start + 32, 0:1024])),
                    reads=(("rs1b", r0), ("rs2b", r0)), writes=(key_out,), tag="rp6")

    for hg in range(4):
        p.dma("pool", (lambda e, hg=hg: e.dma_start(
            out=w_hg[:].rearrange("p (c n) -> p c n", n=256),
            in_=wkT[:, hg * 256:(hg + 1) * 256].rearrange("(c p) n -> p c n", p=128))),
            writes=(("w_hg",),), tag="wk")
        for m2 in range(2):
            for kt in range(8):
                def mmk(e, m2=m2, kt=kt):
                    r = None
                    for c in range(8):
                        r = e.matmul(pw[kt % 2][:], w_hg[:, c * 256 + m2 * 128: c * 256 + (m2 + 1) * 128],
                                     xlnT[:, c * T_FULL + kt * 512: c * T_FULL + kt * 512 + 512],
                                     start=(c == 0), stop=(c == 7))
                    return r
                p.op("pe", mmk, reads=(("w_hg",),) + XLNT_KEYS, writes=(("pw", kt % 2),), tag="mmk")
                p.op("act", (lambda e, m2=m2, kt=kt, hg=hg: e.activation(
                    kT[:, m2 * T_FULL + kt * 512: m2 * T_FULL + kt * 512 + 512],
                    pw[kt % 2][:], AF.Identity, bias=bk_sb[:, hg * 2 + m2: hg * 2 + m2 + 1])),
                    reads=(("pw", kt % 2), ("bk",)),
                    writes=(("kTr", m2, kt), ("kTroped", m2)), tag="kevac")
            emit_rope(kT, m2, T_FULL, 0, ("kTr", m2, 7), ("kTroped", m2))
        p.dma("pool", (lambda e, hg=hg: e.dma_start(
            out=w_hg[:].rearrange("p (c n) -> p c n", n=256),
            in_=wqT[:, hg * 256:(hg + 1) * 256].rearrange("(c p) n -> p c n", p=128))),
            reads=tuple(("kTroped", m2) for m2 in range(2)),
            writes=(("w_hg",),), tag="wq")
        for m2 in range(2):
            for qt2 in range(4):
                def mmq(e, m2=m2, qt2=qt2):
                    r = None
                    for c in range(8):
                        r = e.matmul(pw[qt2 % 2][:], w_hg[:, c * 256 + m2 * 128: c * 256 + (m2 + 1) * 128],
                                     xlnT[:, c * T_FULL + qt2 * 512: c * T_FULL + qt2 * 512 + 512],
                                     start=(c == 0), stop=(c == 7))
                    return r
                p.op("pe", mmq, reads=(("w_hg",),) + XLNT_KEYS, writes=(("pw", qt2 % 2),), tag="mmq")
                p.op("act", (lambda e, m2=m2, qt2=qt2, hg=hg: e.activation(
                    qT[:, m2 * T_OWN + qt2 * 512: m2 * T_OWN + qt2 * 512 + 512],
                    pw[qt2 % 2][:], AF.Identity, bias=bq_sb[:, hg * 2 + m2: hg * 2 + m2 + 1])),
                    reads=(("pw", qt2 % 2), ("bq",)),
                    writes=(("qTr", m2, qt2), ("qTroped", m2)), tag="qevac")
            emit_rope(qT, m2, T_OWN, 0, ("qTr", m2, 3), ("qTroped", m2))
        p.dma("pool", (lambda e, hg=hg: e.dma_start(
            out=w_hg[:].rearrange("p (c n) -> p c n", n=256),
            in_=wvT[:, hg * 256:(hg + 1) * 256].rearrange("(c p) n -> p c n", p=128))),
            reads=tuple(("qTroped", m2) for m2 in range(2)),
            writes=(("w_hg",),), tag="wv")
        for kb in range(32):
            def mmv(e, kb=kb, hg=hg):
                r = None
                for c in range(8):
                    r = e.matmul(pw[kb % 2][:, 0:256],
                                 xlnT[:, c * T_FULL + kb * 128: c * T_FULL + kb * 128 + 128],
                                 w_hg[:, c * 256:(c + 1) * 256], start=(c == 0), stop=False)
                r = e.matmul(pw[kb % 2][:, 0:256], P.ones1[0:1, :],
                             bv_r[0:1, hg * 256:(hg + 1) * 256], start=False, stop=True)
                return r
            p.op("pe", mmv, reads=(("w_hg",), ("bv",), ("ones1",)) + XLNT_KEYS,
                 writes=(("pw", kb % 2),), tag="mmv")
            p.op("act", (lambda e, kb=kb: e.activation(
                v_sb[:, kb * 260:(kb + 1) * 260].rearrange("p (j n) -> p j n", j=4)[:, :, 0:64],
                pw[kb % 2][:, 0:256].rearrange("p (j n) -> p j n", j=4), AF.Copy)),
                reads=(("pw", kb % 2), ("vones",)), writes=(("v", kb),), tag="vevac")
        p.op("dve", (lambda e: e.memset(sm_sb[0:1, 0:2], 0.0)), writes=(("smqk",),), tag="b0z")
        for which, nslices, toks in (("q", 4, T_OWN), ("k", 8, T_FULL)):
            for m2 in range(2):
                for sl in range(nslices):
                    sqk = ("sq",)
                    src = qT if which == "q" else kT
                    srck = ("qTroped", m2) if which == "q" else ("kTroped", m2)
                    p.op("act", (lambda e, src=src, m2=m2, sl=sl, toks=toks: e.activation(
                        sq_sb[:], src[:, m2 * toks + sl * 512: m2 * toks + sl * 512 + 512],
                        AF.Square)),
                        reads=(srck,), writes=(sqk,), tag="b0sq")
                    for jh in range(2):
                        def mmn(e, jh=jh):
                            return e.matmul(pB[0:1, 0:512], onescb[jh * 64:(jh + 1) * 64, 0:1],
                                            sq_sb[jh * 64:(jh + 1) * 64, :], start=True, stop=True)
                        p.op("pe", mmn, reads=(sqk, ("onescb",)), writes=(("pBn",),), tag="b0mm")
                        p.op("dve", (lambda e: e.reduce_max(sm_sb[0:1, 4:5], pB[0:1, 0:512], axis=AX.X)),
                             reads=(("pBn",),), writes=(("smt",),), tag="b0red")
                        widx = 0 if which == "q" else 1
                        p.op("dve", (lambda e, widx=widx: e.tensor_max(
                            sm_sb[0:1, widx:widx + 1], sm_sb[0:1, widx:widx + 1], sm_sb[0:1, 4:5])),
                            reads=(("smt",), ("smqk",)), writes=(("smqk",),), tag="b0max")
        p.op("dve", (lambda e: e.tensor_mul(sm_sb[0:1, 5:6], sm_sb[0:1, 0:1], sm_sb[0:1, 1:2])),
             reads=(("smqk",),), writes=(("smp",),), tag="b0mul")
        p.op("act", (lambda e: e.activation(sm_sb[0:1, 6:7], sm_sb[0:1, 5:6], AF.Sqrt)),
             reads=(("smp",),), writes=(("smsq",),), tag="b0sqrt")
        p.op("dve", (lambda e: e.tensor_scalar(
            sm_sb[0:1, 7:8], sm_sb[0:1, 6:7], P.cval(-C_ATT)[0:1, :], None, ALU.mult)),
            reads=(("smsq",), ("cvals",)), writes=(("smb0",),), tag="b0fin")
        p.op("pe", (lambda e: e.matmul(pB[:, 0:1], P.ones1[0:1, :], sm_sb[0:1, 7:8],
                                       start=True, stop=True)),
             reads=(("smb0",), ("ones1",)), writes=(("pBn",),), tag="b0bc")
        p.op("dve", (lambda e: e.tensor_copy(nb0_sb[:], pB[:, 0:1])),
             reads=(("pBn",),), writes=(("nb0",),), tag="b0ev")
        KT_KEYS = tuple(("kTroped", m2) for m2 in range(2))
        QT_KEYS = tuple(("qTroped", m2) for m2 in range(2))
        V_KEYS = tuple(("v", kb) for kb in range(32))
        for qt in range(4):
            for kc in range(32):
                for j in range(4):
                    m2, rb = j // 2, (j % 2) * 64
                    ps_s = pw[(kc * 4 + j) % 2]
                    psk = ("pw", (kc * 4 + j) % 2)
                    def mms(e, m2=m2, rb=rb, ps_s=ps_s, kc=kc, qt=qt):
                        return e.matmul(ps_s[:],
                                        kT[rb:rb + 64, m2 * T_FULL + kc * 128: m2 * T_FULL + kc * 128 + 128],
                                        qT[rb:rb + 64, m2 * T_OWN + qt * 512: m2 * T_OWN + qt * 512 + 512],
                                        start=True, stop=True)
                    p.op("pe", mms, reads=KT_KEYS + QT_KEYS, writes=(psk,), tag="mms")
                    tslot = (kc * 4 + j) % 2
                    p.op("act", (lambda e, ps_s=ps_s, tslot=tslot: e.activation(
                        p_sb[:, tslot * 512:(tslot + 1) * 512], ps_s[:],
                        AF.Exp, bias=nb0_sb[:], scale=C_ATT)),
                        reads=(psk, ("nb0",)), writes=(("p", tslot),), tag="pexp")
                    def mmpv(e, j=j, kc=kc, tslot=tslot):
                        return e.matmul(acc[j][:], v_sb[:, kc * 260 + j * 65: kc * 260 + j * 65 + 65],
                                        p_sb[:, tslot * 512:(tslot + 1) * 512],
                                        start=(kc == 0), stop=(kc == 31))
                    p.op("pe", mmpv, reads=(("p", tslot),) + V_KEYS, writes=(("acc", j),), tag="mmpv")
            for j in range(4):
                jg = hg * 4 + j
                p.op("dve", (lambda e, j=j: e.reciprocal(rdrow[:], acc[j][64:65, :])),
                     reads=(("acc", j),), writes=(("rdrow",),), tag="recip")
                p.op("pe", (lambda e: e.matmul(pB[0:64, :], P.ones1[0:1, 0:64], rdrow[0:1, :],
                                               start=True, stop=True)),
                     reads=(("rdrow",), ("ones1",)), writes=(("pBn",),), tag="rdbc")
                p.op("act", (lambda e: e.activation(rdB[:], pB[0:64, :], AF.Copy)),
                     reads=(("pBn",),), writes=(("rdB",),), tag="rdev")
                p.op("dve", (lambda e, j=j, jg=jg, qt=qt: e.tensor_mul(
                    yT[(jg % 2) * 64:(jg % 2) * 64 + 64,
                       (jg // 2) * T_OWN + qt * 512:(jg // 2) * T_OWN + qt * 512 + 512],
                    acc[j][0:64, :], rdB[:])),
                    reads=(("acc", j), ("rdB",)), writes=(("yT", jg, qt),), tag="ydiv")
    YT_KEYS = tuple(("yT", jg, qt) for jg in range(16) for qt in range(4))
    for dhalf in range(2):
        p.dma("pool", (lambda e, dhalf=dhalf: e.dma_start(
            out=wo_sb[:].rearrange("p (c n) -> p c n", n=512),
            in_=woT[:, dhalf * 512:(dhalf + 1) * 512].rearrange("(c p) n -> p c n", p=128))),
            writes=(("wo",),), tag="wo")
        for tb in range(16):
            po = pw[tb % 2]
            psk = ("pw", tb % 2)
            def mmo(e, po=po, tb=tb, dhalf=dhalf):
                r = None
                for c in range(8):
                    r = e.matmul(po[:], yT[:, c * T_OWN + tb * 128: c * T_OWN + tb * 128 + 128],
                                 wo_sb[:, c * 512:(c + 1) * 512], start=(c == 0), stop=False)
                r = e.matmul(po[:], P.ones1[0:1, :], bo_r[0:1, dhalf * 512:(dhalf + 1) * 512],
                             start=False, stop=True)
                return r
            p.op("pe", mmo, reads=(("wo",), ("bo",), ("ones1",)) + YT_KEYS, writes=(psk,), tag="mmo")
            ek = ("xepi",)
            p.dma("sp", (lambda e, tb=tb: e.dma_start(
                out=xepi[:], in_=xin[tb * 128:(tb + 1) * 128, :])),
                writes=(ek,), tag="ldxe")
            owk = ("xepi",)
            p.op("dve", (lambda e, po=po, dhalf=dhalf: e.tensor_add(
                xepi[:, dhalf * 512:(dhalf + 1) * 512], po[:],
                xepi[:, dhalf * 512:(dhalf + 1) * 512])),
                reads=(psk, ek), writes=(owk,), tag="oepi")
            p.dma("sp", (lambda e, tb=tb, dhalf=dhalf: e.dma_start(
                out=xout[tb * 128:(tb + 1) * 128, dhalf * 512:(dhalf + 1) * 512],
                in_=xepi[:, dhalf * 512:(dhalf + 1) * 512])),
                reads=(owk,), writes=(("xepi",),), tag="stx")

    P.finish()
    return P


def _rope_l3_tables(kpos):
    i = np.arange(0, 64, 2, dtype=np.float32)
    inv = (np.float32(1.0) / np.power(np.float32(10000.0), (i / np.float32(64.0)))).astype(np.float32)
    ang = kpos.astype(np.float32)[None, :] * inv[:, None]
    c = np.cos(ang).astype(np.float32)
    s = np.sin(ang).astype(np.float32)
    ct = np.tile(c, (4, 1)).astype(BFNP)
    st = np.tile(s, (4, 1)).astype(BFNP)
    return ct, st


def _rope_perm_cols():
    perm = []
    for h in range(16):
        base = h * 64
        perm.extend(base + np.arange(0, 64, 2))
        perm.extend(base + np.arange(1, 64, 2))
    return np.array(perm)


def stage_L3a_inputs(inputs, core):
    b, h = core // 2, core % 2
    x = inputs["_x_cur"][b]
    xperm = np.concatenate([x[h * T_OWN:(h + 1) * T_OWN], x[(1 - h) * T_OWN:(2 - h) * T_OWN]], 0)
    kpos = np.concatenate([np.arange(h * T_OWN, (h + 1) * T_OWN),
                           np.arange((1 - h) * T_OWN, (2 - h) * T_OWN)]).astype(np.float32)
    perm = _rope_perm_cols()
    ct, st = _rope_l3_tables(kpos)
    wq = np.asarray(inputs["L3_attn_Wq"])
    wk = np.asarray(inputs["L3_attn_Wk"])
    return {
        "xin": f32c(xperm),
        "wqT": f32c(wq.T[:, perm]),
        "wkT": f32c(wk.T[:, perm]),
        "wvT": f32c(np.asarray(inputs["L3_attn_Wv"]).T),
        "woT": f32c(np.asarray(inputs["L3_attn_Wo"]).T),
        "bq128": f32c(np.asarray(inputs["L3_attn_bq"])[perm].reshape(8, 128).T),
        "bk128": f32c(np.asarray(inputs["L3_attn_bk"])[perm].reshape(8, 128).T),
        "bvrow": f32c(np.asarray(inputs["L3_attn_bv"]).reshape(1, D)),
        "borow": f32c(np.asarray(inputs["L3_attn_bo"]).reshape(1, D)),
        "ctab": np.ascontiguousarray(ct),
        "stab": np.ascontiguousarray(st),
    }




DBG_L2A = False
DBG_L2A_NOROPE = False


def build_L2a():
    P = Part()
    p = P.p
    xin = P.din("xin", [T_FULL, D])
    wqT = P.din("wqT", [D, D])
    wkT = P.din("wkT", [D, D])
    wvT = P.din("wvT", [D, D])
    woT = P.din("woT", [D, D])
    bq128 = P.din("bq128", [128, 8])
    bkrow = P.din("bkrow", [1, D])
    bvrow = P.din("bvrow", [1, D])
    borow = P.din("borow", [1, D])
    ctqd = P.din("ctq", [128, T_OWN], BF16)
    stqd = P.din("stq", [128, T_OWN], BF16)
    ctkd = P.din("ctk", [T_FULL, 128], BF16)
    stkd = P.din("stk", [T_FULL, 128], BF16)
    xout = P.dout("xout", [T_OWN, D])

    common_consts(P)
    lnb = make_lnb(P)
    xlnT = P.sb("xlnT", [128, 8 * T_FULL], BF16)
    qT = P.sb("qT", [128, 2 * T_OWN], BF16)
    yT = P.sb("yT", [128, 8 * T_OWN], BF16)
    w_hg = P.sb("w_hg", [128, 8 * 256], BF16)
    w_hg2 = P.sb("w_hg2", [128, 8 * 256], BF16)
    wo_sb = P.sb("wo_sb", [128, 8 * 512], BF16)
    ctq = P.sb("ctq_sb", [128, T_OWN], BF16)
    stq = P.sb("stq_sb", [128, T_OWN], BF16)
    ctk = P.sb("ctk_sb", [128, 32 * 128], BF16)
    stk = P.sb("stk_sb", [128, 32 * 128], BF16)
    k_scr = P.sb("k_scr", [128, 2 * 256], BF16)
    v_scr = P.sb("v_scr", [128, 2 * 256], BF16)
    kr4 = P.sb("kr4", [128, 512], BF16)
    rs1 = P.sb("rs1", [128, 1024], BF16)
    rs2 = P.sb("rs2", [128, 1024], BF16)
    g_sb = P.sb("g_sb", [128, 128], BF16)
    bq_sb = P.sb("bq_sb", [128, 8])
    bk_r = P.sb("bk_r", [1, D])
    bv_r = P.sb("bv_r", [1, D])
    bo_r = P.sb("bo_r", [1, D])
    xepi = P.sb("xepi", [128, 1024])

    pw = [P.ps(f"pw{i}", [128, 512]) for i in range(2)]
    pG = P.ps("pG", [64, 256])
    pY = P.ps("pY", [64, 512])
    if DBG_L2A:
        dbg_g = P.dout("dbg_g", [128, 128])
        dbg_q = P.dout("dbg_q", [128, 2 * T_OWN])
        dbg_k = P.dout("dbg_k", [128, 256])
        dbg_x = P.dout("dbg_x", [128, T_FULL])
        g_f32 = P.sb("g_f32", [128, 128])
        q_f32 = P.sb("q_f32", [128, 512], BF16)

    for nm, sbuf, dram in (("bq", bq_sb, bq128), ("bk", bk_r, bkrow), ("bv", bv_r, bvrow),
                           ("bo", bo_r, borow), ("ctq", ctq, ctqd), ("stq", stq, stqd)):
        p.dma("sp", (lambda e, sbuf=sbuf, dram=dram: e.dma_start(out=sbuf[:], in_=dram[:])),
              writes=((nm,),), tag="ld" + nm)
    p.dma("sp", (lambda e: e.dma_start(
        out=ctk[:].rearrange("p (kb f) -> p kb f", f=128),
        in_=ctkd.rearrange("(kb p) f -> p kb f", p=128))), writes=(("ctk",),), tag="ldctk")
    p.dma("sp", (lambda e: e.dma_start(
        out=stk[:].rearrange("p (kb f) -> p kb f", f=128),
        in_=stkd.rearrange("(kb p) f -> p kb f", p=128))), writes=(("stk",),), tag="ldstk")

    emit_ln(P, xin, 0, 32, xlnT, T_FULL, lnb)

    def emit_rope_q(m2, key_in, key_out):
        for half in range(2):
            c0 = half * 1024
            for r0 in (0, 64):
                x1 = slice(r0, r0 + 32)
                x2 = slice(r0 + 32, r0 + 64)
                dd = lambda sl, m2=m2, c0=c0: qT[sl, m2 * T_OWN + c0: m2 * T_OWN + c0 + 1024]
                tt = lambda sl, t, c0=c0: t[sl, c0: c0 + 1024]
                p.op("dve", (lambda e, x1=x1, dd=dd, tt=tt: e.tensor_mul(
                    rs1[x1.start:x1.start + 32, 0:1024], dd(x1), tt(x1, ctq))),
                    reads=(key_in, key_out, ("ctq",)), writes=(("rs1", r0),), tag="rq1")
                p.op("dve", (lambda e, x1=x1, x2=x2, dd=dd, tt=tt: e.tensor_mul(
                    rs2[x1.start:x1.start + 32, 0:1024], dd(x2), tt(x2, stq))),
                    reads=(key_in, key_out, ("stq",)), writes=(("rs2", r0),), tag="rq2")
                p.op("dve", (lambda e, x1=x1, x2=x2, dd=dd, tt=tt: e.tensor_mul(
                    rs1[x2.start:x2.start + 32, 0:1024], dd(x1), tt(x1, stq))),
                    reads=(key_in, key_out, ("stq",)), writes=(("rs1b", r0),), tag="rq3")
                p.op("dve", (lambda e, x2=x2, dd=dd, tt=tt: e.tensor_mul(
                    rs2[x2.start:x2.start + 32, 0:1024], dd(x2), tt(x2, ctq))),
                    reads=(key_in, key_out, ("ctq",)), writes=(("rs2b", r0),), tag="rq4")
                p.op("dve", (lambda e, x1=x1, dd=dd: e.tensor_sub(
                    dd(x1), rs1[x1.start:x1.start + 32, 0:1024],
                    rs2[x1.start:x1.start + 32, 0:1024])),
                    reads=(("rs1", r0), ("rs2", r0)), writes=(key_out,), tag="rq5")
                p.op("dve", (lambda e, x2=x2, dd=dd: e.tensor_add(
                    dd(x2), rs1[x2.start:x2.start + 32, 0:1024],
                    rs2[x2.start:x2.start + 32, 0:1024])),
                    reads=(("rs1b", r0), ("rs2b", r0)), writes=(key_out,), tag="rq6")

    for hg in range(4):
        p.dma("pool", (lambda e, hg=hg: e.dma_start(
            out=w_hg[:].rearrange("p (c n) -> p c n", n=256),
            in_=wqT[:, hg * 256:(hg + 1) * 256].rearrange("(c p) n -> p c n", p=128))),
            writes=(("w_hg",),), tag="wq")
        for m2 in range(2):
            for qt2 in range(4):
                def mmq(e, m2=m2, qt2=qt2):
                    r = None
                    for c in range(8):
                        r = e.matmul(pw[qt2 % 2][:], w_hg[:, c * 256 + m2 * 128: c * 256 + (m2 + 1) * 128],
                                     xlnT[:, c * T_FULL + qt2 * 512: c * T_FULL + qt2 * 512 + 512],
                                     start=(c == 0), stop=(c == 7))
                    return r
                p.op("pe", mmq, reads=(("w_hg",),) + XLNT_KEYS, writes=(("pw", qt2 % 2),), tag="mmq")
                p.op("act", (lambda e, m2=m2, qt2=qt2, hg=hg: e.activation(
                    qT[:, m2 * T_OWN + qt2 * 512: m2 * T_OWN + qt2 * 512 + 512],
                    pw[qt2 % 2][:], AF.Identity, bias=bq_sb[:, hg * 2 + m2: hg * 2 + m2 + 1])),
                    reads=(("pw", qt2 % 2), ("bq",)),
                    writes=(("qTr", m2, qt2), ("qTroped", m2)), tag="qevac")
            if not DBG_L2A_NOROPE:
                emit_rope_q(m2, ("qTr", m2, 3), ("qTroped", m2))
        p.dma("pool", (lambda e, hg=hg: e.dma_start(
            out=w_hg[:].rearrange("p (c n) -> p c n", n=256),
            in_=wkT[:, hg * 256:(hg + 1) * 256].rearrange("(c p) n -> p c n", p=128))),
            reads=tuple(("qTroped", m2) for m2 in range(2)), writes=(("w_hg",),), tag="wk")
        p.dma("pool", (lambda e, hg=hg: e.dma_start(
            out=w_hg2[:].rearrange("p (c n) -> p c n", n=256),
            in_=wvT[:, hg * 256:(hg + 1) * 256].rearrange("(c p) n -> p c n", p=128))),
            writes=(("w_hg2",),), tag="wv")
        for kb in range(32):
            sl = kb % 2
            def mmk(e, kb=kb, hg=hg):
                r = None
                for c in range(8):
                    r = e.matmul(pw[0][:, 0:256],
                                 xlnT[:, c * T_FULL + kb * 128: c * T_FULL + kb * 128 + 128],
                                 w_hg[:, c * 256:(c + 1) * 256], start=(c == 0), stop=False)
                r = e.matmul(pw[0][:, 0:256], P.ones1[0:1, :],
                             bk_r[0:1, hg * 256:(hg + 1) * 256], start=False, stop=True)
                return r
            p.op("pe", mmk, reads=(("w_hg",), ("bk",), ("ones1",)) + XLNT_KEYS,
                 writes=(("pw", 0),), tag="mmk")
            p.op("act", (lambda e, sl=sl: e.activation(
                k_scr[:, sl * 256:(sl + 1) * 256], pw[0][:, 0:256], AF.Copy)),
                reads=(("pw", 0),), writes=(("k_scr", sl),), tag="kevac")
            kv = lambda sl, o: k_scr[:, sl * 256:(sl + 1) * 256].rearrange(
                "p (j f) -> p j f", f=64)[:, :, o:o + 32]
            cv = lambda t, kb, o: t[:, kb * 128:(kb + 1) * 128].rearrange(
                "p (j f) -> p j f", f=32) if o == 0 else t[:, kb * 128:(kb + 1) * 128].rearrange(
                "p (j f) -> p j f", f=32)
            t1 = lambda i: kr4[:, i * 128:(i + 1) * 128].rearrange("p (j f) -> p j f", f=32)
            if not DBG_L2A_NOROPE:
                p.op("dve", (lambda e, sl=sl, kb=kb: e.tensor_mul(t1(0), kv(sl, 0), cv(ctk, kb, 0))),
                     reads=(("k_scr", sl), ("ctk",)), writes=(("kr", 0),), tag="rk1")
                p.op("dve", (lambda e, sl=sl, kb=kb: e.tensor_mul(t1(1), kv(sl, 32), cv(stk, kb, 0))),
                     reads=(("k_scr", sl), ("stk",)), writes=(("kr", 1),), tag="rk2")
                p.op("dve", (lambda e, sl=sl, kb=kb: e.tensor_mul(t1(2), kv(sl, 0), cv(stk, kb, 0))),
                     reads=(("k_scr", sl), ("stk",)), writes=(("kr", 2),), tag="rk3")
                p.op("dve", (lambda e, sl=sl, kb=kb: e.tensor_mul(t1(3), kv(sl, 32), cv(ctk, kb, 0))),
                     reads=(("k_scr", sl), ("ctk",)), writes=(("kr", 3),), tag="rk4")
                p.op("dve", (lambda e, sl=sl: e.tensor_sub(kv(sl, 0), t1(0), t1(1))),
                     reads=(("kr", 0), ("kr", 1)), writes=(("k_scr", sl),), tag="rk5")
                p.op("dve", (lambda e, sl=sl: e.tensor_add(kv(sl, 32), t1(2), t1(3))),
                     reads=(("kr", 2), ("kr", 3)), writes=(("k_scr", sl),), tag="rk6")
            def mmv(e, kb=kb, hg=hg):
                r = None
                for c in range(8):
                    r = e.matmul(pw[1][:, 0:256],
                                 xlnT[:, c * T_FULL + kb * 128: c * T_FULL + kb * 128 + 128],
                                 w_hg2[:, c * 256:(c + 1) * 256], start=(c == 0), stop=False)
                r = e.matmul(pw[1][:, 0:256], P.ones1[0:1, :],
                             bv_r[0:1, hg * 256:(hg + 1) * 256], start=False, stop=True)
                return r
            p.op("pe", mmv, reads=(("w_hg2",), ("bv",), ("ones1",)) + XLNT_KEYS,
                 writes=(("pw", 1),), tag="mmv")
            p.op("act", (lambda e, sl=sl: e.activation(
                v_scr[:, sl * 256:(sl + 1) * 256], pw[1][:, 0:256], AF.Copy)),
                reads=(("pw", 1),), writes=(("v_scr", sl),), tag="vevac")
            def mmg(e, kb=kb, sl=sl):
                r = None
                for j in range(4):
                    r = e.matmul(pG[:, j * 64:(j + 1) * 64],
                                 k_scr[:, sl * 256 + j * 64: sl * 256 + j * 64 + 64],
                                 v_scr[:, sl * 256 + j * 64: sl * 256 + j * 64 + 64],
                                 start=(kb == 0 and j == 0), stop=(kb == 31))
                return r
            p.op("pe", mmg, reads=(("k_scr", sl), ("v_scr", sl)), writes=(("pG",),), tag="mmg")
        for j in range(4):
            rb, cb = (j % 2) * 64, (j // 2) * 64
            p.op("act", (lambda e, j=j, rb=rb, cb=cb: e.activation(
                g_sb[rb:rb + 64, cb:cb + 64], pG[:, j * 64:(j + 1) * 64], AF.Copy, scale=0.125)),
                reads=(("pG",),), writes=(("g", j),), tag="gevac")
        if DBG_L2A and hg == 0:
            p.op("dve", (lambda e: e.tensor_copy(g_f32[:], g_sb[:])),
                 reads=tuple(("g", j) for j in range(4)), writes=(("dbgg",),), tag="dbg")
            p.dma("sp", (lambda e: e.dma_start(out=dbg_g[:], in_=g_f32[:])),
                  reads=(("dbgg",),), tag="dbg")
            p.dma("pool", (lambda e: e.dma_start(out=dbg_q[:], in_=qT[:])),
                  reads=tuple(("qTroped", m2) for m2 in range(2)), tag="dbg")
            p.dma("pool", (lambda e: e.dma_start(out=dbg_k[:], in_=k_scr[:, 256:512])),
                  reads=(("k_scr", 1),), tag="dbg")
            p.dma("pool", (lambda e: e.dma_start(out=dbg_x[:], in_=xlnT[:, 0:T_FULL])),
                  reads=XLNT_KEYS, tag="dbg")
        for qt in range(4):
            for j in range(4):
                jg = hg * 4 + j
                rb, cb = (j % 2) * 64, (j // 2) * 64
                def mmy(e, j=j, rb=rb, cb=cb, qt=qt):
                    return e.matmul(pY[:], g_sb[rb:rb + 64, cb:cb + 64],
                                    qT[rb:rb + 64, (j // 2) * T_OWN + qt * 512:
                                       (j // 2) * T_OWN + qt * 512 + 512],
                                    start=True, stop=True)
                p.op("pe", mmy, reads=(("g", j),) + tuple(("qTroped", m2) for m2 in range(2)),
                     writes=(("pY",),), tag="mmy")
                p.op("act", (lambda e, jg=jg, qt=qt: e.activation(
                    yT[(jg % 2) * 64:(jg % 2) * 64 + 64,
                       (jg // 2) * T_OWN + qt * 512:(jg // 2) * T_OWN + qt * 512 + 512],
                    pY[:], AF.Copy)),
                    reads=(("pY",),), writes=(("yT", jg, qt),), tag="yevac")
    YT_KEYS = tuple(("yT", jg, qt) for jg in range(16) for qt in range(4))
    for dhalf in range(2):
        p.dma("pool", (lambda e, dhalf=dhalf: e.dma_start(
            out=wo_sb[:].rearrange("p (c n) -> p c n", n=512),
            in_=woT[:, dhalf * 512:(dhalf + 1) * 512].rearrange("(c p) n -> p c n", p=128))),
            writes=(("wo",),), tag="wo")
        for tb in range(16):
            po = pw[tb % 2]
            psk = ("pw", tb % 2)
            def mmo(e, po=po, tb=tb, dhalf=dhalf):
                r = None
                for c in range(8):
                    r = e.matmul(po[:], yT[:, c * T_OWN + tb * 128: c * T_OWN + tb * 128 + 128],
                                 wo_sb[:, c * 512:(c + 1) * 512], start=(c == 0), stop=False)
                r = e.matmul(po[:], P.ones1[0:1, :], bo_r[0:1, dhalf * 512:(dhalf + 1) * 512],
                             start=False, stop=True)
                return r
            p.op("pe", mmo, reads=(("wo",), ("bo",), ("ones1",)) + YT_KEYS, writes=(psk,), tag="mmo")
            ek = ("xepi",)
            p.dma("sp", (lambda e, tb=tb: e.dma_start(
                out=xepi[:], in_=xin[tb * 128:(tb + 1) * 128, :])),
                writes=(ek,), tag="ldxe")
            p.op("dve", (lambda e, po=po, dhalf=dhalf: e.tensor_add(
                xepi[:, dhalf * 512:(dhalf + 1) * 512], po[:],
                xepi[:, dhalf * 512:(dhalf + 1) * 512])),
                reads=(psk, ek), writes=(ek,), tag="oepi")
            p.dma("sp", (lambda e, tb=tb, dhalf=dhalf: e.dma_start(
                out=xout[tb * 128:(tb + 1) * 128, dhalf * 512:(dhalf + 1) * 512],
                in_=xepi[:, dhalf * 512:(dhalf + 1) * 512])),
                reads=(ek,), writes=(ek,), tag="stx")

    P.finish()
    return P


def _rope_l2_tables(pos):
    sc = np.arange(0, 64, 2, dtype=np.float32)[:32]
    with np.errstate(over="ignore"):
        freqs = (np.float32(1.0) / np.power(np.float32(10000.0), sc)).astype(np.float32)
    ang = pos.astype(np.float32)[None, :] * freqs[:, None]
    return np.cos(ang).astype(np.float32), np.sin(ang).astype(np.float32)


def stage_L2a_inputs(inputs, core):
    b, h = core // 2, core % 2
    x = inputs["_x_cur"][b]
    xperm = np.concatenate([x[h * T_OWN:(h + 1) * T_OWN], x[(1 - h) * T_OWN:(2 - h) * T_OWN]], 0)
    kpos = np.concatenate([np.arange(h * T_OWN, (h + 1) * T_OWN),
                           np.arange((1 - h) * T_OWN, (2 - h) * T_OWN)]).astype(np.float32)
    qpos = kpos[:T_OWN]
    perm = _rope_perm_cols()
    cq, sq = _rope_l2_tables(qpos)
    ck, sk = _rope_l2_tables(kpos)
    ctq = np.tile(cq, (4, 1)).astype(BFNP)
    stq = np.tile(sq, (4, 1)).astype(BFNP)
    ctk = np.tile(ck.T, (1, 4)).astype(BFNP)
    stk = np.tile(sk.T, (1, 4)).astype(BFNP)
    return {
        "xin": f32c(xperm),
        "wqT": f32c(np.asarray(inputs["L2_attn_Wq"])[:, perm]),
        "wkT": f32c(np.asarray(inputs["L2_attn_Wk"])[:, perm]),
        "wvT": f32c(np.asarray(inputs["L2_attn_Wv"])),
        "woT": f32c(np.asarray(inputs["L2_attn_Wo"])),
        "bq128": f32c(np.asarray(inputs["L2_attn_bq"])[perm].reshape(8, 128).T),
        "bkrow": f32c(np.asarray(inputs["L2_attn_bk"])[perm].reshape(1, D)),
        "bvrow": f32c(np.asarray(inputs["L2_attn_bv"]).reshape(1, D)),
        "borow": f32c(np.asarray(inputs["L2_attn_bo"]).reshape(1, D)),
        "ctq": np.ascontiguousarray(ctq),
        "stq": np.ascontiguousarray(stq),
        "ctk": np.ascontiguousarray(ctk),
        "stk": np.ascontiguousarray(stk),
    }




def build_L1f():
    P = Part()
    p = P.p
    SU = 1024
    NTB = SU // 128
    xin = P.din("xin", [T_OWN, D])
    centT = P.din("centT", [D, 8])
    scl128 = P.din("scl128", [128, 8])
    upw = P.din("upw", [8, D, 4096])
    dnw = P.din("dnw", [8, 2048, D])
    xout = P.dout("xout", [T_OWN, D])
    mscr = P.nc.dram_tensor("mscr", [8, 8 * 128], F32, kind="Internal").ap()

    common_consts(P)
    lnb = make_lnb(P)
    P.onescb = P.load_const_sb("onescb", np.ones((128, 1)).astype(BFNP))
    P.onescol = P.load_const_sb("onescol", np.ones((128, 1), np.float32))
    P.identf = P.load_const_sb("identf", np.eye(128).astype(np.float32), key=("identf",))
    xlnK = P.sb("xlnK", [128, 2 * 1024])
    est = P.sb("est", [128, 12])
    emv = P.sb("emv", [128, 2])
    erst = P.sb("erst", [128, 2])
    skp = P.sb("skp", [128, 1024], BF16)
    xn_tm = P.sb("xn_tm", [128, 2 * 1024])
    xnT32 = P.sb("xnT32", [128, 8 * 128])
    xnT = P.sb("xnT", [128, 8 * SU], BF16)
    rmsa = P.sb("rmsa", [128, 2 * 4], F32)
    rmss = P.sb("rmss", [128, 1024], BF16)
    cent_sb = P.sb("cent_sb", [128, 8 * 8])
    csq = P.sb("csq", [1, 16])
    sq8 = P.sb("sq8", [128, 8])
    scl_sb = P.sb("scl_sb", [128, 8])
    m1_sb = P.sb("m1_sb", [128, 4])
    mask_tm = P.sb("mask_tm", [128, 8])
    mrow = P.sb("mrow", [8, NTB * 128])
    mflat = P.sb("mflat", [1, 8 * SU])
    maskB = P.sb("maskB", [128, SU], BF16)
    a_sb = P.sb("a_sb", [128, 16 * SU], BF16)
    silg = P.sb("silg", [128, 2 * SU], BF16)
    wup = [P.sb(f"wup{i}", [128, 8 * 128], BF16) for i in range(2)]
    wdn = P.sb("wdn", [128, 16 * 512], BF16)
    out_acc = P.sb("out_acc", [128, NTB * 1024])
    xepi = P.sb("xepi", [128, 1024])
    pw = [P.ps(f"pw{i}", [128, 512]) for i in range(2)]
    pB = P.ps("pB", [128, 512])
    pGt = P.ps("pGt", [128, 512])
    pT32 = P.ps("pT32", [128, 512])

    p.dma("sp", (lambda e: e.dma_start(
        out=cent_sb[:].rearrange("p (c n) -> p c n", n=8),
        in_=centT.rearrange("(c p) n -> p c n", p=128))), writes=(("cent",),), tag="ldcent")
    p.dma("sp", (lambda e: e.dma_start(out=scl_sb[:], in_=scl128[:])), writes=(("scl",),), tag="ldscl")
    for c in range(8):
        p.op("act", (lambda e, c=c: e.activation(sq8[:], cent_sb[:, c * 8:(c + 1) * 8], AF.Square)),
             reads=(("cent",),), writes=(("sq8",),), tag="csqsq")
        p.op("pe", (lambda e, c=c: e.matmul(pB[0:1, 0:8], P.onescol[:, 0:1], sq8[:],
                                            start=(c == 0), stop=(c == 7))),
             reads=(("sq8",), ("onescol",)), writes=(("pB",),), tag="csqmm")
    p.op("dve", (lambda e: e.tensor_scalar(csq[0:1, 0:8], pB[0:1, 0:8], P.cval(-0.5)[0:1, :],
                                           None, ALU.mult)),
         reads=(("pB",), ("cvals",)), writes=(("csq",),), tag="csqfin")

    H_KEYS = tuple(("h", m, tt) for m in range(16) for tt in range(2))

    for su in range(2):
        tok0 = su * SU
        for b in range(NTB):
            s = b % 2
            xs = lnb["x_sb"][:, s * 1024:(s + 1) * 1024]
            xk = ("lnx", s)
            p.dma("sp", (lambda e, xs=xs, b=b, tok0=tok0: e.dma_start(
                out=xs, in_=xin[tok0 + b * 128: tok0 + (b + 1) * 128, :])), writes=(xk,), tag="ldx")
            p.op("dve", (lambda e, xs=xs, s=s: (
                e.bn_stats(lnb["stats"][:, s * 12: s * 12 + 6], xs[:, 0:512]),
                e.bn_stats(lnb["stats"][:, s * 12 + 6: s * 12 + 12], xs[:, 512:1024]))[-1]),
                reads=(xk,), writes=(("lnst6", s),), tag="bnstats")
            p.op("dve", (lambda e, s=s: e.bn_aggr(
                lnb["mv"][:, s * 2: s * 2 + 2], lnb["stats"][:, s * 12: s * 12 + 12])),
                reads=(("lnst6", s),), writes=(("lnst", s),), tag="bnaggr")
            p.op("act", (lambda e, s=s: e.activation(
                lnb["rstd"][:, s * 2: s * 2 + 1], lnb["mv"][:, s * 2 + 1: s * 2 + 2],
                AF.Sqrt, bias=P.cval(1e-5), scale=1.0)),
                reads=(("lnst", s), ("cvals",)), writes=(("lnsq", s),), tag="lnsqrt")
            p.op("dve", (lambda e, s=s: e.reciprocal(
                lnb["rstd"][:, s * 2 + 1: s * 2 + 2], lnb["rstd"][:, s * 2: s * 2 + 1])),
                reads=(("lnsq", s),), writes=(("lnrs", s),), tag="lnrecip")
            p.op("dve", (lambda e, xs=xs, s=s: e.tensor_scalar(
                xlnK[:, s * 1024:(s + 1) * 1024], xs,
                lnb["mv"][:, s * 2: s * 2 + 1], lnb["rstd"][:, s * 2 + 1: s * 2 + 2],
                ALU.subtract, ALU.mult)),
                reads=(xk, ("lnrs", s)), writes=(("xlnK", s),), tag="lnnorm")
            p.op("act", (lambda e, s=s: e.activation(
                rmss[:], xlnK[:, s * 1024:(s + 1) * 1024], AF.Square,
                accum_out=rmsa[:, s * 4: s * 4 + 1])),
                reads=(("xlnK", s),), writes=(("rmss",), ("rmsa", s)), tag="rmssq")
            p.op("act", (lambda e, s=s: e.activation(
                rmsa[:, s * 4 + 1: s * 4 + 2], rmsa[:, s * 4: s * 4 + 1],
                AF.Sqrt, bias=P.cval(1e-6), scale=1.0 / 1024.0)),
                reads=(("rmsa", s), ("cvals",)), writes=(("rmsb", s),), tag="rmssqrt")
            p.op("dve", (lambda e, s=s: e.reciprocal(
                rmsa[:, s * 4 + 2: s * 4 + 3], rmsa[:, s * 4 + 1: s * 4 + 2])),
                reads=(("rmsb", s),), writes=(("rmsc", s),), tag="rmsrecip")
            p.op("dve", (lambda e, s=s: e.tensor_scalar(
                xn_tm[:, s * 1024:(s + 1) * 1024], xlnK[:, s * 1024:(s + 1) * 1024],
                rmsa[:, s * 4 + 2: s * 4 + 3], None, ALU.mult)),
                reads=(("xlnK", s), ("rmsc", s)), writes=(("xn_tm", s),), tag="rmsnorm")
            for g in range(2):
                p.op("pe", (lambda e, s=s, g=g: [
                    P.nc.tensor.transpose(
                        pT32[:, i * 128:(i + 1) * 128],
                        xn_tm[:, s * 1024 + (g * 4 + i) * 128: s * 1024 + (g * 4 + i + 1) * 128],
                        P.identf[:])
                    for i in range(4)][-1]),
                    reads=(("xn_tm", s), ("identf",)), writes=(("pT32",),), tag="lntr")
                p.op("dve", (lambda e, b=b, g=g: [
                    e.tensor_scalar(
                        xnT[:, (g * 4 + i) * SU + b * 128: (g * 4 + i) * SU + b * 128 + 128],
                        pT32[:, i * 128:(i + 1) * 128],
                        scl_sb[:, g * 4 + i: g * 4 + i + 1], None, ALU.mult)
                    for i in range(4)][-1]),
                    reads=(("pT32",), ("scl",)), writes=(("xnT", b, g),), tag="lnevac")
                p.op("dve", (lambda e, g=g: [
                    e.tensor_scalar(
                        xnT32[:, (g * 4 + i) * 128:(g * 4 + i + 1) * 128],
                        pT32[:, i * 128:(i + 1) * 128],
                        scl_sb[:, g * 4 + i: g * 4 + i + 1], None, ALU.mult)
                    for i in range(4)][-1]),
                    reads=(("pT32",), ("scl",)), writes=(("xnT32", g),), tag="lnevac32")
            def mmgate(e, b=b):
                r = None
                for c in range(8):
                    r = e.matmul(pGt[:, 0:8], xnT32[:, c * 128:(c + 1) * 128],
                                 cent_sb[:, c * 8:(c + 1) * 8], start=(c == 0), stop=False)
                r = e.matmul(pGt[:, 0:8], P.ones1[0:1, :], csq[0:1, 0:8], start=False, stop=True)
                return r
            p.op("pe", mmgate, reads=(("cent",), ("csq",), ("ones1",), ("xnT32", 0), ("xnT32", 1)),
                 writes=(("pGt",),), tag="mmgate")
            p.op("dve", (lambda e: e.reduce_max(m1_sb[:, 0:1], pGt[:, 0:8], axis=AX.X)),
                 reads=(("pGt",),), writes=(("m1",),), tag="gmax")
            p.op("dve", (lambda e: e.tensor_scalar(
                mask_tm[:], pGt[:, 0:8], m1_sb[:, 0:1], None, ALU.is_ge)),
                reads=(("pGt",), ("m1",)), writes=(("mask_tm",),), tag="gmask")
            p.op("pe", (lambda e: P.nc.tensor.transpose(
                pGt[0:8, 64:192], mask_tm[:], P.identf[:])),
                reads=(("mask_tm",), ("identf",)), writes=(("pGt",),), tag="mtr")
            p.op("dve", (lambda e, b=b: e.tensor_copy(mrow[:, b * 128:(b + 1) * 128],
                                                      pGt[0:8, 64:192])),
                reads=(("pGt",),), writes=(("mrow", b),), tag="mev")
        XNT_KEYS = tuple(("xnT", b, g) for b in range(NTB) for g in range(2))
        p.dma("pool", (lambda e: e.dma_start(out=mscr[:, :], in_=mrow[:])),
              reads=tuple(("mrow", tb) for tb in range(NTB)), writes=(("mscr",),), tag="mdma1")
        p.dma("pool", (lambda e: e.dma_start(
            out=mflat[:], in_=mscr.rearrange("q n -> (q n)")[None, :])),
            reads=(("mscr",),), writes=(("mflat",),), tag="mdma2")
        for e_i in range(8):
            p.op("pe", (lambda e, e_i=e_i: e.matmul(
                pB[:],
                P.ones1[0:1, :], mflat[0:1, e_i * SU: e_i * SU + 512],
                start=True, stop=True)),
                reads=(("mflat",), ("ones1",)), writes=(("pB",),), tag="mbc")
            p.op("pe", (lambda e, e_i=e_i: e.matmul(
                pGt[:], P.ones1[0:1, :], mflat[0:1, e_i * SU + 512:(e_i + 1) * SU],
                start=True, stop=True)),
                reads=(("mflat",), ("ones1",)), writes=(("pGt",),), tag="mbc2")
            p.op("act", (lambda e: e.activation(maskB[:, 0:512], pB[:], AF.Copy)),
                 reads=(("pB",),), writes=(("maskB",),), tag="mbev")
            p.op("act", (lambda e: e.activation(maskB[:, 512:1024], pGt[:], AF.Copy)),
                 reads=(("pGt",),), writes=(("maskB",),), tag="mbev2")
            for m in range(32):
                wb = wup[m % 2]
                kw = ("wup", m % 2)
                p.dma("pool", (lambda e, wb=wb, m=m, e_i=e_i: e.dma_start(
                    out=wb[:].rearrange("p (c n) -> p c n", n=128),
                    in_=upw[e_i, :, m * 128:(m + 1) * 128].rearrange("(c p) n -> p c n", p=128))),
                    writes=(kw,), tag="upw")
                for tt in range(2):
                    ps_ = pw[tt]
                    psk = ("pw", tt)
                    def mmu(e, wb=wb, ps_=ps_, tt=tt):
                        r = None
                        for c in range(8):
                            r = e.matmul(ps_[:], wb[:, c * 128:(c + 1) * 128],
                                         xnT[:, c * SU + tt * 512: c * SU + tt * 512 + 512],
                                         start=(c == 0), stop=(c == 7))
                        return r
                    p.op("pe", mmu, reads=(kw,) + XNT_KEYS, writes=(psk,), tag="mmu")
                    if m < 16:
                        p.op("act", (lambda e, m=m, tt=tt, ps_=ps_: e.activation(
                            a_sb[:, m * SU + tt * 512: m * SU + tt * 512 + 512], ps_[:], AF.Copy)),
                            reads=(psk,), writes=(("h", m, tt),), tag="aev")
                    else:
                        p.op("act", (lambda e, tt=tt, ps_=ps_: e.activation(
                            silg[:, tt * 512:(tt + 1) * 512], ps_[:], AF.Silu)),
                            reads=(psk,), writes=(("silg", tt),), tag="gev")
                        p.op("dve", (lambda e, m=m, tt=tt: e.tensor_mul(
                            silg[:, SU + tt * 512: SU + tt * 512 + 512],
                            silg[:, tt * 512:(tt + 1) * 512],
                            a_sb[:, (m - 16) * SU + tt * 512: (m - 16) * SU + tt * 512 + 512])),
                            reads=(("silg", tt), ("h", m - 16, tt)), writes=(("u", tt),), tag="uev")
                        p.op("dve", (lambda e, m=m, tt=tt: e.tensor_mul(
                            a_sb[:, (m - 16) * SU + tt * 512: (m - 16) * SU + tt * 512 + 512],
                            silg[:, SU + tt * 512: SU + tt * 512 + 512], maskB[:, tt * 512:(tt + 1) * 512])),
                            reads=(("u", tt), ("maskB",)), writes=(("h", m - 16, tt),), tag="umev")
            for half in range(2):
                p.dma("pool", (lambda e, e_i=e_i, half=half: e.dma_start(
                    out=wdn[:].rearrange("p (c n) -> p c n", n=512),
                    in_=dnw[e_i, :, half * 512:(half + 1) * 512].rearrange("(c p) n -> p c n", p=128))),
                    writes=(("wdn",),), tag="dnw")
                for tb in range(NTB):
                    ps_ = pw[tb % 2]
                    psk = ("pw", tb % 2)
                    def mmd(e, ps_=ps_, tb=tb):
                        r = None
                        for c in range(16):
                            r = e.matmul(ps_[:], a_sb[:, c * SU + tb * 128: c * SU + tb * 128 + 128],
                                         wdn[:, c * 512:(c + 1) * 512], start=(c == 0), stop=(c == 15))
                        return r
                    p.op("pe", mmd, reads=(("wdn",),) + H_KEYS, writes=(psk,), tag="mmd")
                    ok = ("oacc", tb, half)
                    if e_i == 0:
                        p.op("dve", (lambda e, ps_=ps_, tb=tb, half=half: e.tensor_copy(
                            out_acc[:, tb * 1024 + half * 512: tb * 1024 + half * 512 + 512], ps_[:])),
                            reads=(psk,), writes=(ok,), tag="oacc0")
                    else:
                        p.op("dve", (lambda e, ps_=ps_, tb=tb, half=half: e.tensor_add(
                            out_acc[:, tb * 1024 + half * 512: tb * 1024 + half * 512 + 512],
                            out_acc[:, tb * 1024 + half * 512: tb * 1024 + half * 512 + 512], ps_[:])),
                            reads=(psk, ok), writes=(ok,), tag="oacc")
        for tb in range(NTB):
            ek = ("xepi",)
            p.dma("sp", (lambda e, tb=tb, tok0=tok0: e.dma_start(
                out=xepi[:], in_=xin[tok0 + tb * 128: tok0 + (tb + 1) * 128, :])),
                writes=(ek,), tag="ldxe")
            p.op("dve", (lambda e: (
                e.bn_stats(est[:, 0:6], xepi[:, 0:512]),
                e.bn_stats(est[:, 6:12], xepi[:, 512:1024]))[-1]),
                reads=(ek,), writes=(("est6",),), tag="ebs")
            p.op("dve", (lambda e: e.bn_aggr(emv[:, 0:2], est[:, 0:12])),
                 reads=(("est6",),), writes=(("est",),), tag="eba")
            p.op("act", (lambda e: e.activation(erst[:, 0:1], emv[:, 1:2], AF.Sqrt,
                                                bias=P.cval(1e-5), scale=1.0)),
                 reads=(("est",), ("cvals",)), writes=(("esq",),), tag="esq")
            p.op("dve", (lambda e: e.reciprocal(erst[:, 1:2], erst[:, 0:1])),
                 reads=(("esq",),), writes=(("ers",),), tag="ers")
            p.op("dve", (lambda e: e.tensor_scalar(
                skp[:], xepi[:], emv[:, 0:1], erst[:, 1:2], ALU.subtract, ALU.mult)),
                reads=(ek, ("ers",)), writes=(("skp",),), tag="eskp")
            p.op("dve", (lambda e, tb=tb: e.tensor_add(
                out_acc[:, tb * 1024:(tb + 1) * 1024],
                out_acc[:, tb * 1024:(tb + 1) * 1024], skp[:])),
                reads=(("skp",), ("oacc", tb, 0), ("oacc", tb, 1)),
                writes=(("oacc", tb, 0), ("oacc", tb, 1)), tag="skipadd")
            p.op("dve", (lambda e, tb=tb: e.tensor_add(
                out_acc[:, tb * 1024:(tb + 1) * 1024],
                out_acc[:, tb * 1024:(tb + 1) * 1024], xepi[:])),
                reads=(ek, ("oacc", tb, 0), ("oacc", tb, 1)),
                writes=(("oacc", tb, 0), ("oacc", tb, 1)), tag="xadd")
            p.dma("sp", (lambda e, tb=tb, tok0=tok0: e.dma_start(
                out=xout[tok0 + tb * 128: tok0 + (tb + 1) * 128, :],
                in_=out_acc[:, tb * 1024:(tb + 1) * 1024])),
                reads=(("oacc", tb, 0), ("oacc", tb, 1)), tag="stx")

    P.finish()
    return P


def stage_L1f_inputs(inputs, core):
    b, h = core // 2, core % 2
    x = inputs["_x_cur"][b]
    return {
        "xin": f32c(x[h * T_OWN:(h + 1) * T_OWN]),
        "centT": f32c(np.asarray(inputs["L1_ffn_centroids"]).T),
        "scl128": f32c(np.asarray(inputs["L1_ffn_scale"]).reshape(8, 128).T),
        "upw": f32c(np.asarray(inputs["L1_ffn_up_w"])),
        "dnw": f32c(np.asarray(inputs["L1_ffn_down_w"])),
    }




DBG_L2F_MODE = 0
DBG_L2F_DUMP = False


def build_L2f():
    P = Part()
    p = P.p
    SU = 1024
    NTB = SU // 128
    xin = P.din("xin", [T_OWN, D])
    wg = P.din("wg", [D, 8])
    wu = P.din("wu", [8, D, 2048])
    wd = P.din("wd", [8, 2048, D])
    w1 = P.din("w1", [D, 2048])
    w2 = P.din("w2", [2048, D])
    xout = P.dout("xout", [T_OWN, D])
    mscr = P.nc.dram_tensor("mscr", [8, 8 * 128], F32, kind="Internal").ap()

    common_consts(P)
    lnb = make_lnb(P)
    P.identf = P.load_const_sb("identf", np.eye(128).astype(np.float32), key=("identf",))
    xlnT = P.sb("xlnT", [128, 8 * SU], BF16)
    xtm32 = P.sb("xtm32", [128, 2 * 1024])
    xlnT32 = P.sb("xlnT32", [128, 8 * 128])
    wg_sb = P.sb("wg_sb", [128, 8 * 8])
    neg_sb = P.sb("neg_sb", [128, 8])
    el_sb = P.sb("el_sb", [128, 8])
    l2_sb = P.sb("l2_sb", [128, 8])
    sel_sb = P.sb("sel_sb", [128, 8])
    mm_sb = P.sb("mm_sb", [128, 8])
    cw_sb = P.sb("cw_sb", [128, 8])
    mk2_sb = P.sb("mk2_sb", [128, 8])
    mrow = P.sb("mrow", [8, NTB * 128])
    mflat = P.sb("mflat", [1, 8 * SU])
    cwB = P.sb("cwB", [128, SU], BF16)
    h_sb = P.sb("h_sb", [128, 16 * SU], BF16)
    wup = [P.sb(f"wup{i}", [128, 8 * 128], BF16) for i in range(2)]
    wdn = P.sb("wdn", [128, 16 * 512], BF16)
    out_acc = P.sb("out_acc", [128, NTB * 1024])
    xepi = P.sb("xepi", [128, 1024])
    pw = [P.ps(f"pw{i}", [128, 512]) for i in range(2)]
    pB = P.ps("pB", [128, 512])
    pGt = P.ps("pGt", [128, 512])
    pT32 = P.ps("pT32", [128, 512])
    if DBG_L2F_DUMP:
        dbg_cw = P.dout("dbg_cw", [1, 8 * 1024])

    p.dma("sp", (lambda e: e.dma_start(
        out=wg_sb[:].rearrange("p (c n) -> p c n", n=8),
        in_=wg.rearrange("(c p) n -> p c n", p=128))), writes=(("wgk",),), tag="ldwg")
    p.op("dve", (lambda e: e.memset(neg_sb[:], -1e30)), writes=(("neg",),), tag="neg")

    H_KEYS = tuple(("h", m, tt) for m in range(16) for tt in range(2))

    for su in range(2):
        tok0 = su * SU
        for b in range(NTB):
            s = b % 2
            xs = lnb["x_sb"][:, s * 1024:(s + 1) * 1024]
            xk = ("lnx", s)
            p.dma("sp", (lambda e, xs=xs, b=b, tok0=tok0: e.dma_start(
                out=xs, in_=xin[tok0 + b * 128: tok0 + (b + 1) * 128, :])), writes=(xk,), tag="ldx")
            p.op("dve", (lambda e, xs=xs, s=s: (
                e.bn_stats(lnb["stats"][:, s * 12: s * 12 + 6], xs[:, 0:512]),
                e.bn_stats(lnb["stats"][:, s * 12 + 6: s * 12 + 12], xs[:, 512:1024]))[-1]),
                reads=(xk,), writes=(("lnst6", s),), tag="bnstats")
            p.op("dve", (lambda e, s=s: e.bn_aggr(
                lnb["mv"][:, s * 2: s * 2 + 2], lnb["stats"][:, s * 12: s * 12 + 12])),
                reads=(("lnst6", s),), writes=(("lnst", s),), tag="bnaggr")
            p.op("act", (lambda e, s=s: e.activation(
                lnb["rstd"][:, s * 2: s * 2 + 1], lnb["mv"][:, s * 2 + 1: s * 2 + 2],
                AF.Sqrt, bias=P.cval(1e-5), scale=1.0)),
                reads=(("lnst", s), ("cvals",)), writes=(("lnsq", s),), tag="lnsqrt")
            p.op("dve", (lambda e, s=s: e.reciprocal(
                lnb["rstd"][:, s * 2 + 1: s * 2 + 2], lnb["rstd"][:, s * 2: s * 2 + 1])),
                reads=(("lnsq", s),), writes=(("lnrs", s),), tag="lnrecip")
            p.op("dve", (lambda e, xs=xs, s=s: e.tensor_scalar(
                xtm32[:, s * 1024:(s + 1) * 1024], xs,
                lnb["mv"][:, s * 2: s * 2 + 1], lnb["rstd"][:, s * 2 + 1: s * 2 + 2],
                ALU.subtract, ALU.mult)),
                reads=(xk, ("lnrs", s)), writes=(("lntm", s),), tag="lnnorm")
            for g in range(2):
                p.op("pe", (lambda e, s=s, g=g: [
                    P.nc.tensor.transpose(
                        pT32[:, i * 128:(i + 1) * 128],
                        xtm32[:, s * 1024 + (g * 4 + i) * 128: s * 1024 + (g * 4 + i + 1) * 128],
                        P.identf[:])
                    for i in range(4)][-1]),
                    reads=(("lntm", s), ("identf",)), writes=(("pT32",),), tag="lntr")
                p.op("dve", (lambda e, b=b, g=g: [
                    e.tensor_copy(
                        xlnT[:, (g * 4 + i) * SU + b * 128: (g * 4 + i) * SU + b * 128 + 128],
                        pT32[:, i * 128:(i + 1) * 128])
                    for i in range(4)][-1]),
                    reads=(("pT32",),), writes=(("xlnT", b, g),), tag="lnevac")
                p.op("dve", (lambda e, g=g: [
                    e.tensor_copy(
                        xlnT32[:, (g * 4 + i) * 128:(g * 4 + i + 1) * 128],
                        pT32[:, i * 128:(i + 1) * 128])
                    for i in range(4)][-1]),
                    reads=(("pT32",),), writes=(("xlnT32", g),), tag="lnevac32")
            def mmgate(e):
                r = None
                for c in range(8):
                    r = e.matmul(pGt[:, 0:8], xlnT32[:, c * 128:(c + 1) * 128],
                                 wg_sb[:, c * 8:(c + 1) * 8], start=(c == 0), stop=(c == 7))
                return r
            p.op("pe", mmgate, reads=(("wgk",), ("xlnT32", 0), ("xlnT32", 1)),
                 writes=(("pGt",),), tag="mmgate")
            p.op("dve", (lambda e: e.reduce_max(mm_sb[:, 0:1], pGt[:, 0:8], axis=AX.X, negate=True)),
                 reads=(("pGt",),), writes=(("nm1",),), tag="gm1")
            p.op("dve", (lambda e: e.tensor_scalar(
                mm_sb[:, 1:2], mm_sb[:, 0:1], P.cval(-1.0), None, ALU.mult)),
                reads=(("nm1",), ("cvals",)), writes=(("m1",),), tag="gm1b")
            p.op("dve", (lambda e: e.tensor_scalar(
                sel_sb[:], pGt[:, 0:8], mm_sb[:, 1:2], None, ALU.is_ge)),
                reads=(("pGt",), ("m1",)), writes=(("mask1",),), tag="gmask1")
            p.op("dve", (lambda e: e.scalar_tensor_tensor(
                l2_sb[:], sel_sb[:], P.cval(-1e30), pGt[:, 0:8], ALU.mult, ALU.add)),
                reads=(("mask1",), ("cvals",), ("pGt",)), writes=(("l2",),), tag="gsel")
            p.op("dve", (lambda e: e.reduce_max(mm_sb[:, 2:3], l2_sb[:], axis=AX.X)),
                 reads=(("l2",),), writes=(("m2",),), tag="gm2")
            p.op("dve", (lambda e: e.tensor_scalar(
                mk2_sb[:], l2_sb[:], mm_sb[:, 2:3], None, ALU.is_ge)),
                reads=(("l2",), ("m2",)), writes=(("mask2",),), tag="gmask2")
            p.op("dve", (lambda e: e.tensor_add(sel_sb[:], sel_sb[:], mk2_sb[:])),
                 reads=(("mask1",), ("mask2",)), writes=(("sel",),), tag="gseladd")
            p.op("act", (lambda e: e.activation(el_sb[:], pGt[:, 0:8], AF.Exp,
                                                bias=mm_sb[:, 0:1])),
                 reads=(("pGt",), ("nm1",)), writes=(("el",),), tag="gexp")
            p.op("act", (lambda e: e.activation(mm_sb[:, 3:4], mm_sb[:, 2:3], AF.Exp,
                                                bias=mm_sb[:, 0:1])),
                 reads=(("m2",), ("nm1",)), writes=(("ed2",),), tag="gexp2")
            p.op("dve", (lambda e: e.tensor_scalar(
                mm_sb[:, 4:5], mm_sb[:, 3:4], P.cval(1.0), None, ALU.add)),
                reads=(("ed2",), ("cvals",)), writes=(("den",),), tag="gden")
            p.op("dve", (lambda e: e.reciprocal(mm_sb[:, 5:6], mm_sb[:, 4:5])),
                 reads=(("den",),), writes=(("rden",),), tag="grden")
            p.op("dve", (lambda e: e.tensor_mul(cw_sb[:], el_sb[:], sel_sb[:])),
                 reads=(("el",), ("sel",)), writes=(("cw0",),), tag="gcw0")
            p.op("dve", (lambda e: e.tensor_scalar(
                cw_sb[:], cw_sb[:], mm_sb[:, 5:6], None, ALU.mult)),
                reads=(("cw0",), ("rden",)), writes=(("cw",),), tag="gcw")
            p.op("pe", (lambda e: P.nc.tensor.transpose(
                pGt[0:8, 64:192], cw_sb[:], P.identf[:])),
                reads=(("cw",), ("identf",)), writes=(("pGt",),), tag="ctr")
            p.op("dve", (lambda e, b=b: e.tensor_copy(mrow[:, b * 128:(b + 1) * 128],
                                                      pGt[0:8, 64:192])),
                reads=(("pGt",),), writes=(("mrow", b),), tag="cev")
        XLNTK = tuple(("xlnT", b, g) for b in range(NTB) for g in range(2))
        p.dma("pool", (lambda e: e.dma_start(out=mscr[:, :], in_=mrow[:])),
              reads=tuple(("mrow", tb) for tb in range(NTB)), writes=(("mscr",),), tag="mdma1")
        p.dma("pool", (lambda e: e.dma_start(
            out=mflat[:], in_=mscr.rearrange("q n -> (q n)")[None, :])),
            reads=(("mscr",),), writes=(("mflat",),), tag="mdma2")
        if DBG_L2F_DUMP and su == 0:
            p.dma("sp", (lambda e: e.dma_start(out=dbg_cw[:], in_=mflat[:])),
                  reads=(("mflat",),), tag="dbg")
        e_list = {0: range(-1, 8), 1: range(-1, 0), 2: range(-1, 8)}[DBG_L2F_MODE]
        for e_i in e_list:
            if e_i >= 0:
                p.op("pe", (lambda e, e_i=e_i: e.matmul(
                    pB[:], P.ones1[0:1, :], mflat[0:1, e_i * SU: e_i * SU + 512],
                    start=True, stop=True)),
                    reads=(("mflat",), ("ones1",)), writes=(("pB",),), tag="mbc")
                p.op("pe", (lambda e, e_i=e_i: e.matmul(
                    pGt[:], P.ones1[0:1, :], mflat[0:1, e_i * SU + 512:(e_i + 1) * SU],
                    start=True, stop=True)),
                    reads=(("mflat",), ("ones1",)), writes=(("pGt",),), tag="mbc2")
                p.op("act", (lambda e: e.activation(cwB[:, 0:512], pB[:], AF.Copy)),
                     reads=(("pB",),), writes=(("cwB",),), tag="mbev")
                p.op("act", (lambda e: e.activation(cwB[:, 512:1024], pGt[:], AF.Copy)),
                     reads=(("pGt",),), writes=(("cwB",),), tag="mbev2")
            for m in range(16):
                wb = wup[m % 2]
                kw = ("wup", m % 2)
                p.dma("pool", (lambda e, wb=wb, m=m, e_i=e_i: e.dma_start(
                    out=wb[:].rearrange("p (c n) -> p c n", n=128),
                    in_=(wu[e_i, :, m * 128:(m + 1) * 128] if e_i >= 0
                         else w1[:, m * 128:(m + 1) * 128]).rearrange("(c p) n -> p c n", p=128))),
                    writes=(kw,), tag="upw")
                for tt in range(2):
                    ps_ = pw[tt]
                    psk = ("pw", tt)
                    def mmu(e, wb=wb, ps_=ps_, tt=tt):
                        r = None
                        for c in range(8):
                            r = e.matmul(ps_[:], wb[:, c * 128:(c + 1) * 128],
                                         xlnT[:, c * SU + tt * 512: c * SU + tt * 512 + 512],
                                         start=(c == 0), stop=(c == 7))
                        return r
                    p.op("pe", mmu, reads=(kw,) + XLNTK, writes=(psk,), tag="mmu")
                    p.op("act", (lambda e, m=m, tt=tt, ps_=ps_: e.activation(
                        h_sb[:, m * SU + tt * 512: m * SU + tt * 512 + 512], ps_[:], AF.Gelu)),
                        reads=(psk,), writes=(("h", m, tt),), tag="hgelu")
                    if e_i >= 0:
                        p.op("dve", (lambda e, m=m, tt=tt: e.tensor_mul(
                            h_sb[:, m * SU + tt * 512: m * SU + tt * 512 + 512],
                            h_sb[:, m * SU + tt * 512: m * SU + tt * 512 + 512],
                            cwB[:, tt * 512:(tt + 1) * 512])),
                            reads=(("h", m, tt), ("cwB",)), writes=(("h", m, tt),), tag="hcw")
            for half in range(2):
                p.dma("pool", (lambda e, e_i=e_i, half=half: e.dma_start(
                    out=wdn[:].rearrange("p (c n) -> p c n", n=512),
                    in_=(wd[e_i, :, half * 512:(half + 1) * 512] if e_i >= 0
                         else w2[:, half * 512:(half + 1) * 512]).rearrange("(c p) n -> p c n", p=128))),
                    writes=(("wdn",),), tag="dnw")
                for tb in range(NTB):
                    ps_ = pw[tb % 2]
                    psk = ("pw", tb % 2)
                    def mmd(e, ps_=ps_, tb=tb):
                        r = None
                        for c in range(16):
                            r = e.matmul(ps_[:], h_sb[:, c * SU + tb * 128: c * SU + tb * 128 + 128],
                                         wdn[:, c * 512:(c + 1) * 512], start=(c == 0), stop=(c == 15))
                        return r
                    p.op("pe", mmd, reads=(("wdn",),) + H_KEYS, writes=(psk,), tag="mmd")
                    ok = ("oacc", tb, half)
                    if e_i == -1:
                        p.op("dve", (lambda e, ps_=ps_, tb=tb, half=half: e.tensor_copy(
                            out_acc[:, tb * 1024 + half * 512: tb * 1024 + half * 512 + 512], ps_[:])),
                            reads=(psk,), writes=(ok,), tag="oacc0")
                    else:
                        p.op("dve", (lambda e, ps_=ps_, tb=tb, half=half: e.tensor_add(
                            out_acc[:, tb * 1024 + half * 512: tb * 1024 + half * 512 + 512],
                            out_acc[:, tb * 1024 + half * 512: tb * 1024 + half * 512 + 512], ps_[:])),
                            reads=(psk, ok), writes=(ok,), tag="oacc")
        for tb in range(NTB):
            ek = ("xepi",)
            p.dma("sp", (lambda e, tb=tb, tok0=tok0: e.dma_start(
                out=xepi[:], in_=xin[tok0 + tb * 128: tok0 + (tb + 1) * 128, :])),
                writes=(ek,), tag="ldxe")
            p.op("dve", (lambda e, tb=tb: e.tensor_add(
                out_acc[:, tb * 1024:(tb + 1) * 1024],
                out_acc[:, tb * 1024:(tb + 1) * 1024], xepi[:])),
                reads=(ek, ("oacc", tb, 0), ("oacc", tb, 1)),
                writes=(("oacc", tb, 0), ("oacc", tb, 1)), tag="xadd")
            p.dma("sp", (lambda e, tb=tb, tok0=tok0: e.dma_start(
                out=xout[tok0 + tb * 128: tok0 + (tb + 1) * 128, :],
                in_=out_acc[:, tb * 1024:(tb + 1) * 1024])),
                reads=(("oacc", tb, 0), ("oacc", tb, 1)), tag="stx")

    P.finish()
    return P


def stage_L2f_inputs(inputs, core):
    b, h = core // 2, core % 2
    x = inputs["_x_cur"][b]
    return {
        "xin": f32c(x[h * T_OWN:(h + 1) * T_OWN]),
        "wg": f32c(np.asarray(inputs["L2_ffn_Wg"])),
        "wu": f32c(np.asarray(inputs["L2_ffn_Wu"])),
        "wd": f32c(np.asarray(inputs["L2_ffn_Wd"])),
        "w1": f32c(np.asarray(inputs["L2_ffn_W1"])),
        "w2": f32c(np.asarray(inputs["L2_ffn_W2"])),
    }




QTHR = [(0.25, 0.5), (0.75, 0.5), (1.25, 0.5), (1.75, 0.5), (2.5, 1.0), (3.5, 1.0), (5.0, 2.0)]


def emit_quant(P, qb, src_ap, dst_ap, W, rkeys, wkey):
    p = P.p
    B = W // 16
    sc, scl, thr, tabs, sg, ind, qacc = (qb["sc"], qb["scl"], qb["thr"], qb["tabs"],
                                         qb["sg"], qb["ind"], qb["qacc"])
    v3 = lambda t, w=W: t[:, 0:w].rearrange("p (b k) -> p b k", k=16)
    b1 = lambda t, b=B: t[:, 0:b].rearrange("p (b o) -> p b o", o=1).broadcast_to([128, b, 16])
    p.op("dve", (lambda e: e.tensor_reduce(sc[:, 0:B], v3(src_ap if False else None) if False else
                                           src_ap.rearrange("p (b k) -> p b k", k=16),
                                           AX.X, ALU.max, apply_absolute_value=True)),
         reads=rkeys, writes=(("q_sc",),), tag="qred")
    p.op("dve", (lambda e: e.tensor_scalar(scl[:, 0:B], sc[:, 0:B], P.cval(1.0 / 6.0),
                                           P.cval(1e-12), ALU.mult, ALU.max)),
         reads=(("q_sc",), ("cvals",)), writes=(("q_scl",),), tag="qscl")
    p.op("act", (lambda e: e.activation(tabs[:, 0:W], src_ap, AF.Abs)),
         reads=rkeys, writes=(("q_tabs",),), tag="qabs")
    p.op("act", (lambda e: e.activation(sg[:, 0:W], src_ap, AF.Sign)),
         reads=rkeys, writes=(("q_sg",),), tag="qsgn")
    for k, (t, w) in enumerate(QTHR):
        p.op("dve", (lambda e, t=t: e.tensor_scalar(thr[:, 0:B], scl[:, 0:B], P.cval(t),
                                                    None, ALU.mult)),
             reads=(("q_scl",), ("cvals",), ("q_ind",)), writes=(("q_thr",),), tag="qthr")
        p.op("dve", (lambda e: e.tensor_tensor(
            ind[:, 0:W].rearrange("p (b k) -> p b k", k=16),
            tabs[:, 0:W].rearrange("p (b k) -> p b k", k=16), b1(thr), ALU.is_gt)),
            reads=(("q_tabs",), ("q_thr",)), writes=(("q_ind",),), tag="qind")
        if k == 0:
            p.op("dve", (lambda e, w=w: e.tensor_scalar(
                qacc[:, 0:W], ind[:, 0:W], P.cval(w), None, ALU.mult)),
                reads=(("q_ind",), ("cvals",)), writes=(("q_acc",),), tag="qacc0")
        else:
            p.op("dve", (lambda e, w=w: e.scalar_tensor_tensor(
                qacc[:, 0:W], ind[:, 0:W], P.cval(w), qacc[:, 0:W], ALU.mult, ALU.add)),
                reads=(("q_ind",), ("q_acc",), ("cvals",)), writes=(("q_acc",),), tag="qacc")
    p.op("dve", (lambda e: e.tensor_mul(qacc[:, 0:W], qacc[:, 0:W], sg[:, 0:W])),
         reads=(("q_acc",), ("q_sg",)), writes=(("q_acc",),), tag="qsg")
    p.op("dve", (lambda e: e.tensor_tensor(
        dst_ap.rearrange("p (b k) -> p b k", k=16),
        qacc[:, 0:W].rearrange("p (b k) -> p b k", k=16), b1(scl), ALU.mult)),
        reads=(("q_acc",), ("q_scl",)), writes=(wkey,), tag="qfin")


DBG_L3F = False


def build_L3f():
    P = Part()
    p = P.p
    SU = 1024
    NTB = SU // 128
    I3 = 2816
    NC3 = I3 // 128
    xin = P.din("xin", [T_OWN, D])
    wgn = P.din("wgn", [I3, D])
    wun = P.din("wun", [I3, D])
    wdnat = P.din("wdnat", [D, I3])
    xout = P.dout("xout", [T_OWN, D])
    wgTq = P.nc.dram_tensor("wgTq", [D, I3], BF16, kind="Internal").ap()
    wuTq = P.nc.dram_tensor("wuTq", [D, I3], BF16, kind="Internal").ap()
    wdTq = P.nc.dram_tensor("wdTq", [I3, D], BF16, kind="Internal").ap()

    common_consts(P)
    lnb = make_lnb(P)
    qb = {
        "sc": P.sb("q_sc", [128, 176]),
        "scl": P.sb("q_scl", [128, 176]),
        "thr": P.sb("q_thr", [128, 176]),
        "tabs": P.sb("q_tabs", [128, I3]),
        "sg": P.sb("q_sg", [128, I3], BF16),
        "ind": P.sb("q_ind", [128, I3], BF16),
        "qacc": P.sb("q_qacc", [128, I3]),
    }
    xtm32 = P.sb("xtm32", [128, 2 * 1024])
    xq_tm = P.sb("xq_tm", [128, 1024], BF16)
    xqT = P.sb("xqT", [128, 8 * SU], BF16)
    hqT = P.sb("hqT", [128, NC3 * SU], BF16)
    h_tm = P.sb("h_tm", [128, I3])
    hq_tm = P.sb("hq_tm", [128, I3], BF16)
    wt_sb = P.sb("wt_sb", [128, I3])
    wq_sb = P.sb("wq_sb", [128, I3], BF16)
    t4 = P.sb("t4", [128, 512], BF16)
    wgm = [P.sb(f"wgm{i}", [128, 8 * 128], BF16) for i in range(2)]
    wum = [P.sb(f"wum{i}", [128, 8 * 128], BF16) for i in range(2)]
    wdn_sb = P.sb("wdn_sb", [128, NC3 * 512], BF16)
    sil = P.sb("sil", [128, 128], BF16)
    xepi = P.sb("xepi", [128, 1024])
    pw = [P.ps(f"pw{i}", [128, 512]) for i in range(2)]
    pg = P.ps("pg", [128, 128])
    py = P.ps("py", [128, 128])
    if DBG_L3F:
        dbg_xq = P.dout("dbg_xq", [128, 1024])
        dbg_h = P.dout("dbg_h", [128, I3])
        dbg_hq = P.dout("dbg_hq", [128, I3])
        dbg_w = P.dout("dbg_w", [128, 1024], BF16)

    def wprep(nm, wnat, wdst, nrow, wid):
        nc_in = wid // 128
        for r in range(nrow):
            p.dma("sp", (lambda e, r=r, wnat=wnat, wid=wid: e.dma_start(
                out=wt_sb[:, 0:wid], in_=wnat[r * 128:(r + 1) * 128, :])),
                reads=(("q_tabs",), ("q_sg",), ("wtq",)), writes=(("wt",),), tag=nm + "ld")
            emit_quant(P, qb, wt_sb[:, 0:wid], wq_sb[:, 0:wid], wid, (("wt",),), ("wtq",))
            for g0 in range(0, nc_in, 4):
                gn = min(4, nc_in - g0)
                p.op("pe", (lambda e, g0=g0, gn=gn: [
                    P.nc.tensor.transpose(
                        lnb["pT"][:, i * 128:(i + 1) * 128],
                        wq_sb[:, (g0 + i) * 128:(g0 + i + 1) * 128], P.identb[:])
                    for i in range(gn)][-1]),
                    reads=(("wtq",), ("ident",), ("t4",)), writes=(("lnpT",),), tag=nm + "tr")
                p.op("dve", (lambda e, gn=gn: e.tensor_copy(t4[:, 0:gn * 128],
                                                            lnb["pT"][:, 0:gn * 128])),
                     reads=(("lnpT",),), writes=(("t4",),), tag=nm + "ev")
                p.dma("pool", (lambda e, r=r, g0=g0, gn=gn, wdst=wdst: e.dma_start(
                    out=wdst[g0 * 128:(g0 + gn) * 128, r * 128:(r + 1) * 128]
                        .rearrange("(g q) n -> q g n", q=128),
                    in_=t4[:, 0:gn * 128].rearrange("q (g n) -> q g n", n=128))),
                    reads=(("t4",),), writes=(("wdst", nm, r, g0),), tag=nm + "st")

    wprep("wg", wgn, wgTq, NC3, D)
    if DBG_L3F:
        p.dma("pool", (lambda e: e.dma_start(out=dbg_w[:], in_=wgTq[0:128, 0:1024])),
              reads=tuple(("wdst", "wg", r, g0) for r in range(NC3) for g0 in range(0, 8, 4)),
              tag="dbgw")
    wprep("wu", wun, wuTq, NC3, D)
    wprep("wd", wdnat, wdTq, D // 128, I3)
    WG_KEYS = tuple(("wdst", "wg", r, g0) for r in range(NC3) for g0 in range(0, 8, 4))
    WU_KEYS = tuple(("wdst", "wu", r, g0) for r in range(NC3) for g0 in range(0, 8, 4))
    WD_KEYS = tuple(("wdst", "wd", r, g0) for r in range(8) for g0 in range(0, NC3, 4))

    for su in range(2):
        tok0 = su * SU
        for b in range(NTB):
            s = b % 2
            xs = lnb["x_sb"][:, s * 1024:(s + 1) * 1024]
            xk = ("lnx", s)
            p.dma("sp", (lambda e, xs=xs, b=b, tok0=tok0: e.dma_start(
                out=xs, in_=xin[tok0 + b * 128: tok0 + (b + 1) * 128, :])), writes=(xk,), tag="ldx")
            p.op("dve", (lambda e, xs=xs, s=s: (
                e.bn_stats(lnb["stats"][:, s * 12: s * 12 + 6], xs[:, 0:512]),
                e.bn_stats(lnb["stats"][:, s * 12 + 6: s * 12 + 12], xs[:, 512:1024]))[-1]),
                reads=(xk,), writes=(("lnst6", s),), tag="bnstats")
            p.op("dve", (lambda e, s=s: e.bn_aggr(
                lnb["mv"][:, s * 2: s * 2 + 2], lnb["stats"][:, s * 12: s * 12 + 12])),
                reads=(("lnst6", s),), writes=(("lnst", s),), tag="bnaggr")
            p.op("act", (lambda e, s=s: e.activation(
                lnb["rstd"][:, s * 2: s * 2 + 1], lnb["mv"][:, s * 2 + 1: s * 2 + 2],
                AF.Sqrt, bias=P.cval(1e-5), scale=1.0)),
                reads=(("lnst", s), ("cvals",)), writes=(("lnsq", s),), tag="lnsqrt")
            p.op("dve", (lambda e, s=s: e.reciprocal(
                lnb["rstd"][:, s * 2 + 1: s * 2 + 2], lnb["rstd"][:, s * 2: s * 2 + 1])),
                reads=(("lnsq", s),), writes=(("lnrs", s),), tag="lnrecip")
            p.op("dve", (lambda e, xs=xs, s=s: e.tensor_scalar(
                xtm32[:, s * 1024:(s + 1) * 1024], xs,
                lnb["mv"][:, s * 2: s * 2 + 1], lnb["rstd"][:, s * 2 + 1: s * 2 + 2],
                ALU.subtract, ALU.mult)),
                reads=(xk, ("lnrs", s)), writes=(("lntm", s),), tag="lnnorm")
            emit_quant(P, qb, xtm32[:, s * 1024:(s + 1) * 1024], xq_tm[:, 0:1024], 1024,
                       (("lntm", s),), ("xq_tm",))
            if DBG_L3F and su == 0 and b == 0:
                p.dma("pool", (lambda e: e.dma_start(out=dbg_xq[:], in_=xq_tm[:, 0:1024])),
                      reads=(("xq_tm",),), tag="dbg")
            for g in range(2):
                p.op("pe", (lambda e, g=g: [
                    P.nc.tensor.transpose(
                        lnb["pT"][:, i * 128:(i + 1) * 128],
                        xq_tm[:, (g * 4 + i) * 128:(g * 4 + i + 1) * 128], P.identb[:])
                    for i in range(4)][-1]),
                    reads=(("xq_tm",), ("ident",)), writes=(("lnpT",),), tag="lntr")
                p.op("dve", (lambda e, b=b, g=g: [
                    e.tensor_copy(
                        xqT[:, (g * 4 + i) * SU + b * 128: (g * 4 + i) * SU + b * 128 + 128],
                        lnb["pT"][:, i * 128:(i + 1) * 128])
                    for i in range(4)][-1]),
                    reads=(("lnpT",),), writes=(("xqT", b, g),), tag="lnevac")
        XQ_KEYS = tuple(("xqT", b, g) for b in range(NTB) for g in range(2))
        for tb in range(NTB):
            for m in range(NC3):
                wb_g, wb_u = wgm[m % 2], wum[m % 2]
                kg, ku = ("wgm", m % 2), ("wum", m % 2)
                if tb == 0 or True:
                    pass
                p.dma("pool", (lambda e, wb_g=wb_g, m=m: e.dma_start(
                    out=wb_g[:].rearrange("p (c n) -> p c n", n=128),
                    in_=wgTq[:, m * 128:(m + 1) * 128].rearrange("(c p) n -> p c n", p=128))),
                    reads=WG_KEYS, writes=(kg,), tag="ldwg")
                p.dma("pool", (lambda e, wb_u=wb_u, m=m: e.dma_start(
                    out=wb_u[:].rearrange("p (c n) -> p c n", n=128),
                    in_=wuTq[:, m * 128:(m + 1) * 128].rearrange("(c p) n -> p c n", p=128))),
                    reads=WU_KEYS, writes=(ku,), tag="ldwu")
                def mmg(e, wb_g=wb_g, tb=tb):
                    r = None
                    for c in range(8):
                        r = e.matmul(pg[:], xqT[:, c * SU + tb * 128: c * SU + tb * 128 + 128],
                                     wb_g[:, c * 128:(c + 1) * 128], start=(c == 0), stop=(c == 7))
                    return r
                p.op("pe", mmg, reads=(kg,) + XQ_KEYS, writes=(("pg",),), tag="mmg")
                def mmy(e, wb_u=wb_u, tb=tb):
                    r = None
                    for c in range(8):
                        r = e.matmul(py[:], xqT[:, c * SU + tb * 128: c * SU + tb * 128 + 128],
                                     wb_u[:, c * 128:(c + 1) * 128], start=(c == 0), stop=(c == 7))
                    return r
                p.op("pe", mmy, reads=(ku,) + XQ_KEYS, writes=(("py",),), tag="mmy")
                p.op("act", (lambda e: e.activation(sil[:], pg[:], AF.Silu)),
                     reads=(("pg",),), writes=(("sil",),), tag="hsilu")
                p.op("dve", (lambda e, m=m: e.tensor_mul(
                    h_tm[:, m * 128:(m + 1) * 128], py[:], sil[:])),
                    reads=(("py",), ("sil",)), writes=(("h_tm",),), tag="hmul")
            if DBG_L3F and su == 0 and tb == 0:
                p.dma("pool", (lambda e: e.dma_start(out=dbg_h[:], in_=h_tm[:, 0:I3])),
                      reads=(("h_tm",),), tag="dbg")
            emit_quant(P, qb, h_tm[:, 0:I3], hq_tm[:, 0:I3], I3, (("h_tm",),), ("hq_tm",))
            if DBG_L3F and su == 0 and tb == 0:
                p.dma("pool", (lambda e: e.dma_start(out=dbg_hq[:], in_=hq_tm[:, 0:I3])),
                      reads=(("hq_tm",),), tag="dbg")
            for g0 in range(0, NC3, 4):
                gn = min(4, NC3 - g0)
                p.op("pe", (lambda e, g0=g0, gn=gn: [
                    P.nc.tensor.transpose(
                        lnb["pT"][:, i * 128:(i + 1) * 128],
                        hq_tm[:, (g0 + i) * 128:(g0 + i + 1) * 128], P.identb[:])
                    for i in range(gn)][-1]),
                    reads=(("hq_tm",), ("ident",)), writes=(("lnpT",),), tag="htr")
                p.op("dve", (lambda e, tb=tb, g0=g0, gn=gn: [
                    e.tensor_copy(
                        hqT[:, (g0 + i) * SU + tb * 128: (g0 + i) * SU + tb * 128 + 128],
                        lnb["pT"][:, i * 128:(i + 1) * 128])
                    for i in range(gn)][-1]),
                    reads=(("lnpT",),), writes=(("hqT", tb, g0),), tag="hev")
        HQ_KEYS = tuple(("hqT", tb, g0) for tb in range(NTB) for g0 in range(0, NC3, 4))
        for dhalf in range(2):
            p.dma("pool", (lambda e, dhalf=dhalf: e.dma_start(
                out=wdn_sb[:].rearrange("p (c n) -> p c n", n=512),
                in_=wdTq[:, dhalf * 512:(dhalf + 1) * 512].rearrange("(c p) n -> p c n", p=128))),
                reads=WD_KEYS, writes=(("wdn",),), tag="lddn")
            for tb in range(NTB):
                ps_ = pw[tb % 2]
                psk = ("pw", tb % 2)
                def mmd(e, ps_=ps_, tb=tb):
                    r = None
                    for c in range(NC3):
                        r = e.matmul(ps_[:], hqT[:, c * SU + tb * 128: c * SU + tb * 128 + 128],
                                     wdn_sb[:, c * 512:(c + 1) * 512],
                                     start=(c == 0), stop=(c == NC3 - 1))
                    return r
                p.op("pe", mmd, reads=(("wdn",),) + HQ_KEYS, writes=(psk,), tag="mmd")
                ek = ("xepi",)
                p.dma("sp", (lambda e, tb=tb, tok0=tok0: e.dma_start(
                    out=xepi[:], in_=xin[tok0 + tb * 128: tok0 + (tb + 1) * 128, :])),
                    writes=(ek,), tag="ldxe")
                p.op("dve", (lambda e, ps_=ps_, dhalf=dhalf: e.tensor_add(
                    xepi[:, dhalf * 512:(dhalf + 1) * 512], ps_[:],
                    xepi[:, dhalf * 512:(dhalf + 1) * 512])),
                    reads=(psk, ek), writes=(ek,), tag="oepi")
                p.dma("sp", (lambda e, tb=tb, tok0=tok0, dhalf=dhalf: e.dma_start(
                    out=xout[tok0 + tb * 128: tok0 + (tb + 1) * 128, dhalf * 512:(dhalf + 1) * 512],
                    in_=xepi[:, dhalf * 512:(dhalf + 1) * 512])),
                    reads=(ek,), writes=(ek,), tag="stx")

    P.finish()
    return P


def stage_L3f_inputs(inputs, core):
    b, h = core // 2, core % 2
    x = inputs["_x_cur"][b]
    return {
        "xin": f32c(x[h * T_OWN:(h + 1) * T_OWN]),
        "wgn": f32c(np.asarray(inputs["L3_ffn_w_gate"])),
        "wun": f32c(np.asarray(inputs["L3_ffn_w_up"])),
        "wdnat": f32c(np.asarray(inputs["L3_ffn_w_down"])),
    }




DBG_L0A = False


def build_L0a():
    P = Part()
    p = P.p
    SC0 = float(1.0 / np.sqrt(96.0))
    xin = P.din("xin", [T_FULL, D])
    wq = P.din("wq", [D, D])
    wqrp = P.din("wqrp", [D, D])
    wkrp = P.din("wkrp", [D, 64])
    wkvd = P.din("wkvd", [D, 256])
    wku = P.din("wku", [256, D])
    wvu = P.din("wvu", [256, D])
    wo = P.din("wo", [D, D])
    lobo = P.din("lobo", [1, 16])
    tcq = P.din("tcq", [128, T_OWN], BF16)
    tsq = P.din("tsq", [128, T_OWN], BF16)
    tck = P.din("tck", [128, T_FULL], BF16)
    tsk = P.din("tsk", [128, T_FULL], BF16)
    cmask = P.din("cmask", [T_FULL, T_OWN], BF16)
    xout = P.dout("xout", [T_OWN, D])

    common_consts(P)
    lnb = make_lnb(P)
    onescb = P.load_const_sb("onescb", np.ones((128, 1)).astype(BFNP))
    xlnT = P.sb("xlnT", [128, 8 * T_OWN], BF16)
    xlntr = P.sb("xlntr", [128, 8 * 128], BF16)
    latT = P.sb("latT", [128, 2 * T_FULL], BF16)
    krsh = P.sb("krsh", [128, T_FULL], BF16)
    kcomb = [P.sb(f"kcomb{j}", [128, T_FULL], BF16) for j in range(2)]
    qcomb = [P.sb(f"qcomb{j}", [128, T_OWN], BF16) for j in range(2)]
    v_sb = P.sb("v_sb", [128, 32 * 130], BF16)
    yT = P.sb("yT", [128, 8 * T_OWN], BF16)
    mrun = [P.sb(f"mrun{j}", [128, 512], BF16) for j in range(2)]
    shf = P.sb("shf", [32, 512], BF16)
    rsc = P.sb("rsc", [128, 512], BF16)
    w_sl = P.sb("w_sl", [128, 2048], BF16)
    w_sl2 = P.sb("w_sl2", [128, 512], BF16)
    wo_sb = P.sb("wo_sb", [128, 8 * 512], BF16)
    msk = P.sb("msk", [128, 2 * 512], BF16)
    p_sb = P.sb("p_sb", [128, 2 * 512], BF16)
    sq_sb = P.sb("sq_sb", [128, 512], BF16)
    sm_sb = P.sb("sm_sb", [1, 16])
    nb0_sb = P.sb("nb0_sb", [128, 1])
    cexp = P.sb("cexp", [1, 16])
    lobo_sb = P.sb("lobo_sb", [1, 16])
    den_sb = P.sb("den_sb", [1, 512])
    dtot = P.sb("dtot", [1, 512])
    rdrow = P.sb("rdrow", [1, 512])
    rdB = P.sb("rdB", [64, 512])
    xepi = P.sb("xepi", [128, 1024])
    if DBG_L0A:
        dbg_x = P.dout("dbg_x", [128, 8 * T_OWN])
        dbg_qc = P.dout("dbg_qc", [128, T_OWN])
        dbg_kc = P.dout("dbg_kc", [128, T_FULL])
        dbg_lat = P.dout("dbg_lat", [128, 2 * T_FULL])
        dbg_y = P.dout("dbg_y", [128, 8 * T_OWN])
    acc = [P.ps(f"acc{j}", [65, 512]) for j in range(2)]
    pw = [P.ps(f"pw{i}", [128, 512]) for i in range(2)]
    pB = P.ps("pB", [128, 512])

    for nm, sbuf, dram in (("tcq", None, None),):
        pass
    for nm, t, dr in (("tcq", tcq, None),):
        pass
    for nm, t_sb, t_d in (("tcq_", None, None),):
        break
    tcq_sb = P.sb("tcq_sb", [128, T_OWN], BF16)
    tsq_sb = P.sb("tsq_sb", [128, T_OWN], BF16)
    tck_sb = P.sb("tck_sb", [128, T_FULL], BF16)
    tsk_sb = P.sb("tsk_sb", [128, T_FULL], BF16)
    for nm, sbuf, dram in (("tcq", tcq_sb, tcq), ("tsq", tsq_sb, tsq),
                           ("tck", tck_sb, tck), ("tsk", tsk_sb, tsk),
                           ("lobo", lobo_sb, lobo)):
        p.dma("sp", (lambda e, sbuf=sbuf, dram=dram: e.dma_start(out=sbuf[:], in_=dram[:])),
              writes=((nm,),), tag="ld" + nm)
    p.op("act", (lambda e: e.activation(cexp[:], lobo_sb[:], AF.Exp)),
         reads=(("lobo",),), writes=(("cexp",),), tag="cexp")
    p.op("dve", (lambda e: e.memset(v_sb[:], 1.0)), writes=(("vones",),), tag="vones")
    p.op("dve", (lambda e: e.memset(krsh[:], 0.0)), writes=(("krz",),), tag="krz")
    for j in range(2):
        p.op("dve", (lambda e, j=j: e.memset(kcomb[j][:], 0.0)), writes=(("kcz", j),), tag="kcz")
        p.op("dve", (lambda e, j=j: e.memset(qcomb[j][:], 0.0)), writes=(("qcz", j),), tag="qcz")

    p.dma("pool", (lambda e: e.dma_start(
        out=w_sl[:, 0:2048].rearrange("p (c n) -> p c n", n=256),
        in_=wkvd.rearrange("(c p) n -> p c n", p=128))), writes=(("w_sl",),), tag="wkvd")
    p.dma("pool", (lambda e: e.dma_start(
        out=w_sl2[:, 0:512].rearrange("p (c n) -> p c n", n=64),
        in_=wkrp.rearrange("(c p) n -> p c n", p=128))), writes=(("w_sl2",),), tag="wkr")
    for b in range(32):
        s = b % 2
        xs = lnb["x_sb"][:, s * 1024:(s + 1) * 1024]
        xk = ("lnx", s)
        p.dma("sp", (lambda e, xs=xs, b=b: e.dma_start(
            out=xs, in_=xin[b * 128:(b + 1) * 128, :])), writes=(xk,), tag="ldx")
        p.op("dve", (lambda e, xs=xs, s=s: (
            e.bn_stats(lnb["stats"][:, s * 12: s * 12 + 6], xs[:, 0:512]),
            e.bn_stats(lnb["stats"][:, s * 12 + 6: s * 12 + 12], xs[:, 512:1024]))[-1]),
            reads=(xk,), writes=(("lnst6", s),), tag="bnstats")
        p.op("dve", (lambda e, s=s: e.bn_aggr(
            lnb["mv"][:, s * 2: s * 2 + 2], lnb["stats"][:, s * 12: s * 12 + 12])),
            reads=(("lnst6", s),), writes=(("lnst", s),), tag="bnaggr")
        p.op("act", (lambda e, s=s: e.activation(
            lnb["rstd"][:, s * 2: s * 2 + 1], lnb["mv"][:, s * 2 + 1: s * 2 + 2],
            AF.Sqrt, bias=P.cval(1e-5), scale=1.0)),
            reads=(("lnst", s), ("cvals",)), writes=(("lnsq", s),), tag="lnsqrt")
        p.op("dve", (lambda e, s=s: e.reciprocal(
            lnb["rstd"][:, s * 2 + 1: s * 2 + 2], lnb["rstd"][:, s * 2: s * 2 + 1])),
            reads=(("lnsq", s),), writes=(("lnrs", s),), tag="lnrecip")
        p.op("dve", (lambda e, xs=xs, s=s: e.tensor_scalar(
            lnb["xln_tm"][:, s * 1024:(s + 1) * 1024], xs,
            lnb["mv"][:, s * 2: s * 2 + 1], lnb["rstd"][:, s * 2 + 1: s * 2 + 2],
            ALU.subtract, ALU.mult)),
            reads=(xk, ("lnrs", s)), writes=(("lntm", s),), tag="lnnorm")
        own = b < 16
        for g in range(2):
            p.op("pe", (lambda e, s=s, g=g: [
                P.nc.tensor.transpose(
                    lnb["pT"][:, i * 128:(i + 1) * 128],
                    lnb["xln_tm"][:, s * 1024 + (g * 4 + i) * 128: s * 1024 + (g * 4 + i + 1) * 128],
                    P.identb[:])
                for i in range(4)][-1]),
                reads=(("lntm", s), ("ident",)), writes=(("lnpT",),), tag="lntr")
            if own:
                p.op("dve", (lambda e, b=b, g=g: [
                    e.tensor_copy(
                        xlnT[:, (g * 4 + i) * T_OWN + b * 128: (g * 4 + i) * T_OWN + b * 128 + 128],
                        lnb["pT"][:, i * 128:(i + 1) * 128])
                    for i in range(4)][-1]),
                    reads=(("lnpT",),), writes=(("xlnT", b, g), ("xltr", g)), tag="lnevac")
            else:
                p.op("dve", (lambda e, g=g: [
                    e.tensor_copy(
                        xlntr[:, (g * 4 + i) * 128:(g * 4 + i + 1) * 128],
                        lnb["pT"][:, i * 128:(i + 1) * 128])
                    for i in range(4)][-1]),
                    reads=(("lnpT",),), writes=(("xltr", g),), tag="lnevactr")
        xsrc = (lambda c, b=b: xlnT[:, c * T_OWN + b * 128: c * T_OWN + b * 128 + 128]) if own \
            else (lambda c: xlntr[:, c * 128:(c + 1) * 128])
        XKEY = (("xlnT", b, 0), ("xlnT", b, 1)) if own else (("xltr", 0), ("xltr", 1))
        for m2 in range(2):
            ps_ = pw[m2]
            psk = ("pw", m2)
            def mml(e, m2=m2, xsrc=xsrc):
                r = None
                for c in range(8):
                    r = e.matmul(ps_[:, 0:128] if False else pw[m2][:, 0:128],
                                 w_sl[:, c * 256 + m2 * 128: c * 256 + (m2 + 1) * 128],
                                 xsrc(c), start=(c == 0), stop=(c == 7))
                return r
            p.op("pe", mml, reads=(("w_sl",),) + XKEY, writes=(psk,), tag="mml")
            p.op("act", (lambda e, m2=m2, b=b: e.activation(
                latT[:, m2 * T_FULL + b * 128: m2 * T_FULL + b * 128 + 128],
                pw[m2][:, 0:128], AF.Copy)),
                reads=(psk,), writes=(("latT", b // 4),), tag="latev")
        def mmkr(e, xsrc=xsrc):
            r = None
            for c in range(8):
                r = e.matmul(pB[0:64, 0:128], w_sl2[:, c * 64:(c + 1) * 64],
                             xsrc(c), start=(c == 0), stop=(c == 7))
            return r
        p.op("pe", mmkr, reads=(("w_sl2",),) + XKEY, writes=(("pBn",),), tag="mmkr")
        kc0 = b * 128
        p.op("dve", (lambda e, kc0=kc0: e.tensor_mul(
            krsh[64:80, kc0:kc0 + 128], pB[0:16, 0:128], tck_sb[0:16, kc0:kc0 + 128])),
            reads=(("pBn",), ("tck",), ("krz",)), writes=(("krsh",),), tag="kr1")
        p.op("dve", (lambda e, kc0=kc0: e.tensor_mul(
            rsc[64:80, 0:128], pB[32:48, 0:128], tsk_sb[32:48, kc0:kc0 + 128])),
            reads=(("pBn",), ("tsk",)), writes=(("rsc",),), tag="kr2")
        p.op("dve", (lambda e, kc0=kc0: e.tensor_sub(
            krsh[64:80, kc0:kc0 + 128], krsh[64:80, kc0:kc0 + 128], rsc[64:80, 0:128])),
            reads=(("rsc",), ("krsh",)), writes=(("krsh",),), tag="kr3")
        p.op("dve", (lambda e, kc0=kc0: e.tensor_mul(
            krsh[96:112, kc0:kc0 + 128], pB[32:48, 0:128], tck_sb[96:112, kc0:kc0 + 128])),
            reads=(("pBn",), ("tck",), ("krz",)), writes=(("krsh",),), tag="kr4")
        p.op("dve", (lambda e, kc0=kc0: e.tensor_mul(
            rsc[96:112, 0:128], pB[0:16, 0:128], tsk_sb[0:16, kc0:kc0 + 128])),
            reads=(("pBn",), ("tsk",)), writes=(("rsc",),), tag="kr5")
        p.op("dve", (lambda e, kc0=kc0: e.tensor_add(
            krsh[96:112, kc0:kc0 + 128], krsh[96:112, kc0:kc0 + 128], rsc[96:112, 0:128])),
            reads=(("rsc",), ("krsh",)), writes=(("krsh",),), tag="kr6")
    LAT_KEYS = tuple(("latT", i) for i in range(8))
    XLNT_OWN_KEYS = tuple(("xlnT", b, g) for b in range(16) for g in range(2))
    KC_KEYS = lambda j: (("kcomb", j),)
    QC_KEYS = lambda j: (("qcomb", j),)
    V_KEYS = tuple(("v", kb) for kb in range(32))

    for hg in range(8):
        h0 = hg * 2
        p.dma("pool", (lambda e, h0=h0: e.dma_start(
            out=w_sl2[:, 0:256].rearrange("p (c n) -> p c n", n=128),
            in_=wku[:, h0 * 64:(h0 + 2) * 64].rearrange("(c p) n -> p c n", p=128))),
            reads=(("w_sl2",),), writes=(("w_sl2",),), tag="wku")
        for kt in range(8):
            ps_ = pw[kt % 2]
            psk = ("pw", kt % 2)
            def mmkc(e, ps_=ps_, kt=kt):
                r = None
                for c in range(2):
                    r = e.matmul(ps_[:], w_sl2[:, c * 128:(c + 1) * 128],
                                 latT[:, c * T_FULL + kt * 512: c * T_FULL + kt * 512 + 512],
                                 start=(c == 0), stop=(c == 1))
                return r
            p.op("pe", mmkc, reads=(("w_sl2",),) + LAT_KEYS, writes=(psk,), tag="mmkc")
            for j in range(2):
                p.op("act", (lambda e, ps_=ps_, j=j, kt=kt: e.activation(
                    kcomb[j][0:64, kt * 512:(kt + 1) * 512], ps_[j * 64:(j + 1) * 64, :],
                    AF.Copy, scale=SC0)),
                    reads=(psk, ("kcz", j)), writes=(("kcomb", j),), tag="kcev")
        for j in range(2):
            p.op("dve", (lambda e, j=j: e.tensor_copy(kcomb[j][64:80, :], krsh[64:80, :])),
                 reads=(("krsh",),), writes=(("kcomb", j),), tag="krcp1")
            p.op("dve", (lambda e, j=j: e.tensor_copy(kcomb[j][96:112, :], krsh[96:112, :])),
                 reads=(("krsh",),), writes=(("kcomb", j),), tag="krcp2")
        p.dma("pool", (lambda e, h0=h0: e.dma_start(
            out=w_sl2[:, 0:256].rearrange("p (c n) -> p c n", n=32) if False else
                w_sl[:, 0:1024].rearrange("p (c n) -> p c n", n=128),
            in_=wq[:, h0 * 64:(h0 + 2) * 64].rearrange("(c p) n -> p c n", p=128))),
            reads=(("w_sl",),), writes=(("w_sl",),), tag="wq")
        for qt in range(4):
            ps_ = pw[qt % 2]
            psk = ("pw", qt % 2)
            def mmqc(e, ps_=ps_, qt=qt):
                r = None
                for c in range(8):
                    r = e.matmul(ps_[:], w_sl[:, c * 128:(c + 1) * 128],
                                 xlnT[:, c * T_OWN + qt * 512: c * T_OWN + qt * 512 + 512],
                                 start=(c == 0), stop=(c == 7))
                return r
            p.op("pe", mmqc, reads=(("w_sl",),) + XLNT_OWN_KEYS, writes=(psk,), tag="mmqc")
            for j in range(2):
                p.op("act", (lambda e, ps_=ps_, j=j, qt=qt: e.activation(
                    qcomb[j][0:64, qt * 512:(qt + 1) * 512], ps_[j * 64:(j + 1) * 64, :],
                    AF.Copy)),
                    reads=(psk, ("qcz", j)), writes=(("qcomb", j),), tag="qcev")
        p.dma("pool", (lambda e, h0=h0: e.dma_start(
            out=w_sl[:, 0:1024].rearrange("p (c n) -> p c n", n=128),
            in_=wqrp[:, h0 * 64:(h0 + 2) * 64].rearrange("(c p) n -> p c n", p=128))),
            reads=(("w_sl",),), writes=(("w_sl",),), tag="wqr")
        for qt in range(4):
            ps_ = pw[qt % 2]
            psk = ("pw", qt % 2)
            def mmqr(e, ps_=ps_, qt=qt):
                r = None
                for c in range(8):
                    r = e.matmul(ps_[:], w_sl[:, c * 128:(c + 1) * 128],
                                 xlnT[:, c * T_OWN + qt * 512: c * T_OWN + qt * 512 + 512],
                                 start=(c == 0), stop=(c == 7))
                return r
            p.op("pe", mmqr, reads=(("w_sl",),) + XLNT_OWN_KEYS, writes=(psk,), tag="mmqr")
            qc0 = qt * 512
            for j in range(2):
                x1b, x2b = j * 64, j * 64 + 32
                p.op("dve", (lambda e, ps_=ps_, j=j, qc0=qc0, x1b=x1b: e.tensor_mul(
                    qcomb[j][64:80, qc0:qc0 + 512], ps_[x1b:x1b + 16, :],
                    tcq_sb[x1b:x1b + 16, qc0:qc0 + 512])),
                    reads=(psk, ("tcq",), ("qcz", j)), writes=(("qcomb", j),), tag="qr1")
                p.op("dve", (lambda e, ps_=ps_, j=j, qc0=qc0, x2b=x2b: e.tensor_mul(
                    rsc[64:80, :], ps_[x2b:x2b + 16, :], tsq_sb[x2b:x2b + 16, qc0:qc0 + 512])),
                    reads=(psk, ("tsq",)), writes=(("rsc",),), tag="qr2")
                p.op("dve", (lambda e, j=j, qc0=qc0: e.tensor_sub(
                    qcomb[j][64:80, qc0:qc0 + 512], qcomb[j][64:80, qc0:qc0 + 512],
                    rsc[64:80, :])),
                    reads=(("rsc",), ("qcomb", j)), writes=(("qcomb", j),), tag="qr3")
                p.op("dve", (lambda e, ps_=ps_, j=j, qc0=qc0, x2b=x2b: e.tensor_mul(
                    qcomb[j][96:112, qc0:qc0 + 512], ps_[x2b:x2b + 16, :],
                    tcq_sb[96:112, qc0:qc0 + 512])),
                    reads=(psk, ("tcq",), ("qcz", j)), writes=(("qcomb", j),), tag="qr4")
                p.op("dve", (lambda e, ps_=ps_, j=j, qc0=qc0, x1b=x1b: e.tensor_mul(
                    rsc[96:112, :], ps_[x1b:x1b + 16, :], tsq_sb[x1b:x1b + 16, qc0:qc0 + 512])),
                    reads=(psk, ("tsq",)), writes=(("rsc",),), tag="qr5")
                p.op("dve", (lambda e, j=j, qc0=qc0: e.tensor_add(
                    qcomb[j][96:112, qc0:qc0 + 512], qcomb[j][96:112, qc0:qc0 + 512],
                    rsc[96:112, :])),
                    reads=(("rsc",), ("qcomb", j)), writes=(("qcomb", j),), tag="qr6")
        p.dma("pool", (lambda e, h0=h0: e.dma_start(
            out=w_sl2[:, 0:256].rearrange("p (c n) -> p c n", n=128),
            in_=wvu[:, h0 * 64:(h0 + 2) * 64].rearrange("(c p) n -> p c n", p=128))),
            reads=(("w_sl2",),), writes=(("w_sl2",),), tag="wvu")
        for kb in range(32):
            ps_ = pw[kb % 2]
            psk = ("pw", kb % 2)
            def mmv(e, ps_=ps_, kb=kb):
                r = None
                for c in range(2):
                    r = e.matmul(ps_[:, 0:128],
                                 latT[:, c * T_FULL + kb * 128: c * T_FULL + kb * 128 + 128],
                                 w_sl2[:, c * 128:(c + 1) * 128], start=(c == 0), stop=(c == 1))
                return r
            p.op("pe", mmv, reads=(("w_sl2",),) + LAT_KEYS, writes=(psk,), tag="mmv")
            p.op("act", (lambda e, ps_=ps_, kb=kb: e.activation(
                v_sb[:, kb * 130:(kb + 1) * 130].rearrange("p (j n) -> p j n", j=2)[:, :, 0:64],
                ps_[:, 0:128].rearrange("p (j n) -> p j n", j=2), AF.Copy)),
                reads=(psk, ("vones",)), writes=(("v", kb),), tag="vev")
        if DBG_L0A and hg == 0:
            p.dma("pool", (lambda e: e.dma_start(out=dbg_x[:], in_=xlnT[:])),
                  reads=XLNT_OWN_KEYS, tag="dbg")
            p.dma("pool", (lambda e: e.dma_start(out=dbg_qc[:], in_=qcomb[0][:])),
                  reads=(("qcomb", 0),), tag="dbg")
            p.dma("pool", (lambda e: e.dma_start(out=dbg_kc[:], in_=kcomb[0][:])),
                  reads=(("kcomb", 0),), tag="dbg")
            p.dma("pool", (lambda e: e.dma_start(out=dbg_lat[:], in_=latT[:])),
                  reads=LAT_KEYS, tag="dbg")
        p.op("dve", (lambda e: e.memset(sm_sb[0:1, 0:2], 0.0)), writes=(("smqk",),), tag="b0z")
        for which, nsl in (("q", 4), ("k", 8)):
            for j in range(2):
                src = qcomb[j] if which == "q" else kcomb[j]
                srck = ("qcomb", j) if which == "q" else ("kcomb", j)
                for sl in range(nsl):
                    p.op("act", (lambda e, src=src, sl=sl: e.activation(
                        sq_sb[:], src[:, sl * 512:(sl + 1) * 512], AF.Square)),
                        reads=(srck,), writes=(("sq",),), tag="b0sq")
                    p.op("pe", (lambda e: e.matmul(pB[0:1, 0:512], onescb[:, 0:1], sq_sb[:],
                                                   start=True, stop=True)),
                         reads=(("sq",), ("onescb",)), writes=(("pBn",),), tag="b0mm")
                    p.op("dve", (lambda e: e.reduce_max(sm_sb[0:1, 4:5], pB[0:1, 0:512], axis=AX.X)),
                         reads=(("pBn",),), writes=(("smt",),), tag="b0red")
                    widx = 0 if which == "q" else 1
                    p.op("dve", (lambda e, widx=widx: e.tensor_max(
                        sm_sb[0:1, widx:widx + 1], sm_sb[0:1, widx:widx + 1], sm_sb[0:1, 4:5])),
                        reads=(("smt",), ("smqk",)), writes=(("smqk",),), tag="b0max")
        p.op("dve", (lambda e: e.tensor_mul(sm_sb[0:1, 5:6], sm_sb[0:1, 0:1], sm_sb[0:1, 1:2])),
             reads=(("smqk",),), writes=(("smp",),), tag="b0mul")
        p.op("act", (lambda e: e.activation(sm_sb[0:1, 6:7], sm_sb[0:1, 5:6], AF.Sqrt)),
             reads=(("smp",),), writes=(("smsq",),), tag="b0sqrt")
        p.op("dve", (lambda e: e.tensor_scalar(
            sm_sb[0:1, 7:8], sm_sb[0:1, 6:7], P.cval(-1.0)[0:1, :], None, ALU.mult)),
            reads=(("smsq",), ("cvals",)), writes=(("smb0",),), tag="b0fin")
        p.op("pe", (lambda e: e.matmul(pB[:, 0:1], P.ones1[0:1, :], sm_sb[0:1, 7:8],
                                       start=True, stop=True)),
             reads=(("smb0",), ("ones1",)), writes=(("pBn",),), tag="b0bc")
        p.op("dve", (lambda e: e.tensor_copy(nb0_sb[:], pB[:, 0:1])),
             reads=(("pBn",),), writes=(("nb0",),), tag="b0ev")
        for qt in range(4):
            for kc in range(32):
                ms = kc % 2
                p.dma("sp", (lambda e, qt=qt, kc=kc, ms=ms: e.dma_start(
                    out=msk[:, ms * 512:(ms + 1) * 512],
                    in_=cmask[kc * 128:(kc + 1) * 128, qt * 512:(qt + 1) * 512])),
                    writes=(("msk", ms),), tag="mskld")
                for j in range(2):
                    ps_ = pw[j]
                    psk = ("pw", j)
                    def mms(e, ps_=ps_, j=j, kc=kc, qt=qt):
                        return e.matmul(ps_[:], kcomb[j][:, kc * 128:(kc + 1) * 128],
                                        qcomb[j][:, qt * 512:(qt + 1) * 512],
                                        start=True, stop=True)
                    p.op("pe", mms, reads=KC_KEYS(j) + QC_KEYS(j), writes=(psk,), tag="mms")
                    p.op("act", (lambda e, ps_=ps_, j=j: e.activation(
                        p_sb[:, j * 512:(j + 1) * 512], ps_[:], AF.Exp, bias=nb0_sb[:])),
                        reads=(psk, ("nb0",)), writes=(("p", j),), tag="pexp")
                    p.op("dve", (lambda e, j=j, ms=ms: e.tensor_mul(
                        p_sb[:, j * 512:(j + 1) * 512], p_sb[:, j * 512:(j + 1) * 512],
                        msk[:, ms * 512:(ms + 1) * 512])),
                        reads=(("p", j), ("msk", ms)), writes=(("p", j),), tag="pmask")
                    if kc == 0:
                        p.op("dve", (lambda e, j=j: e.tensor_copy(
                            mrun[j][:], p_sb[:, j * 512:(j + 1) * 512])),
                            reads=(("p", j),), writes=(("mrun", j),), tag="mr0")
                    else:
                        p.op("dve", (lambda e, j=j: e.tensor_max(
                            mrun[j][:], mrun[j][:], p_sb[:, j * 512:(j + 1) * 512])),
                            reads=(("p", j), ("mrun", j)), writes=(("mrun", j),), tag="mrx")
                    def mmpv(e, j=j, kc=kc):
                        return e.matmul(acc[j][:], v_sb[:, kc * 130 + j * 65: kc * 130 + j * 65 + 65],
                                        p_sb[:, j * 512:(j + 1) * 512],
                                        start=(kc == 0), stop=(kc == 31))
                    p.op("pe", mmpv, reads=(("p", j),) + V_KEYS, writes=(("acc", j),), tag="mmpv")
            for j in range(2):
                hglob = h0 + j
                for qsrc in (64, 96, 32):
                    p.op("dve", (lambda e, j=j, qsrc=qsrc: e.tensor_copy(
                        shf[0:32, :], mrun[j][qsrc:qsrc + 32, :])),
                        reads=(("mrun", j),), writes=(("shf",),), tag="mpc")
                    p.op("dve", (lambda e, j=j: e.tensor_max(
                        mrun[j][0:32, :], mrun[j][0:32, :], shf[0:32, :])),
                        reads=(("shf",), ("mrun", j)), writes=(("mrun", j),), tag="mpm")
                sh = 16
                while sh >= 1:
                    mask_perm = [(i + sh) % 32 for i in range(32)]
                    p.op("dve", (lambda e, j=j, mask_perm=mask_perm: e.stream_shuffle(
                        shf[0:32, :], mrun[j][0:32, :], mask_perm)),
                        reads=(("mrun", j),), writes=(("shf",),), tag="mps")
                    p.op("dve", (lambda e, j=j: e.tensor_max(
                        mrun[j][0:32, :], mrun[j][0:32, :], shf[0:32, :])),
                        reads=(("shf",), ("mrun", j)), writes=(("mrun", j),), tag="mpm2")
                    sh //= 2
                p.op("dve", (lambda e, j=j: e.tensor_copy(den_sb[:], acc[j][64:65, :])),
                     reads=(("acc", j),), writes=(("den",),), tag="denev")
                p.op("dve", (lambda e, j=j, hglob=hglob: e.scalar_tensor_tensor(
                    dtot[:], mrun[j][0:1, :], cexp[0:1, hglob:hglob + 1], den_sb[:],
                    ALU.mult, ALU.add)),
                    reads=(("mrun", j), ("cexp",), ("den",)), writes=(("dtot",),), tag="dtot")
                p.op("dve", (lambda e: e.reciprocal(rdrow[:], dtot[:])),
                     reads=(("dtot",),), writes=(("rdrow",),), tag="recip")
                p.op("pe", (lambda e: e.matmul(pB[0:64, :], P.ones1[0:1, 0:64], rdrow[0:1, :],
                                               start=True, stop=True)),
                     reads=(("rdrow",), ("ones1",)), writes=(("pBn",),), tag="rdbc")
                p.op("act", (lambda e: e.activation(rdB[:], pB[0:64, :], AF.Copy)),
                     reads=(("pBn",),), writes=(("rdB",),), tag="rdev")
                p.op("dve", (lambda e, j=j, hglob=hglob, qt=qt: e.tensor_mul(
                    yT[(hglob % 2) * 64:(hglob % 2) * 64 + 64,
                       (hglob // 2) * T_OWN + qt * 512:(hglob // 2) * T_OWN + qt * 512 + 512],
                    acc[j][0:64, :], rdB[:])),
                    reads=(("acc", j), ("rdB",)), writes=(("yT", hglob, qt),), tag="ydiv")
    YT_KEYS = tuple(("yT", jg, qt) for jg in range(16) for qt in range(4))
    if DBG_L0A:
        p.dma("pool", (lambda e: e.dma_start(out=dbg_y[:], in_=yT[:])), reads=YT_KEYS, tag="dbg")
    for dhalf in range(2):
        p.dma("pool", (lambda e, dhalf=dhalf: e.dma_start(
            out=wo_sb[:].rearrange("p (c n) -> p c n", n=512),
            in_=wo[:, dhalf * 512:(dhalf + 1) * 512].rearrange("(c p) n -> p c n", p=128))),
            writes=(("wo",),), tag="wo")
        for tb in range(16):
            po = pw[tb % 2]
            psk = ("pw", tb % 2)
            def mmo(e, po=po, tb=tb):
                r = None
                for c in range(8):
                    r = e.matmul(po[:], yT[:, c * T_OWN + tb * 128: c * T_OWN + tb * 128 + 128],
                                 wo_sb[:, c * 512:(c + 1) * 512], start=(c == 0), stop=(c == 7))
                return r
            p.op("pe", mmo, reads=(("wo",),) + YT_KEYS, writes=(psk,), tag="mmo")
            ek = ("xepi",)
            p.dma("sp", (lambda e, tb=tb: e.dma_start(
                out=xepi[:], in_=xin[tb * 128:(tb + 1) * 128, :])),
                writes=(ek,), tag="ldxe")
            p.op("dve", (lambda e, po=po, dhalf=dhalf: e.tensor_add(
                xepi[:, dhalf * 512:(dhalf + 1) * 512], po[:],
                xepi[:, dhalf * 512:(dhalf + 1) * 512])),
                reads=(psk, ek), writes=(ek,), tag="oepi")
            p.dma("sp", (lambda e, tb=tb, dhalf=dhalf: e.dma_start(
                out=xout[tb * 128:(tb + 1) * 128, dhalf * 512:(dhalf + 1) * 512],
                in_=xepi[:, dhalf * 512:(dhalf + 1) * 512])),
                reads=(ek,), writes=(ek,), tag="stx")

    P.finish()
    return P


def _rope_l0_tables(pos, scale):
    inv = (np.float32(1.0) / np.power(np.float32(10000.0),
                                      (np.arange(16, dtype=np.float32) / np.float32(16.0))))
    ang = pos.astype(np.float32)[None, :] * inv[:, None].astype(np.float32)
    c = (np.cos(ang) * scale).astype(BFNP)
    s = (np.sin(ang) * scale).astype(BFNP)
    C = np.zeros((128, len(pos)), BFNP)
    S = np.zeros((128, len(pos)), BFNP)
    for r0 in (0, 32, 64, 96):
        C[r0:r0 + 16] = c
        S[r0:r0 + 16] = s
    return C, S


def stage_L0a_inputs(inputs, core):
    b, h = core // 2, core % 2
    x = inputs["_x_cur"][b]
    xperm = np.concatenate([x[h * T_OWN:(h + 1) * T_OWN], x[(1 - h) * T_OWN:(2 - h) * T_OWN]], 0)
    kpos = np.concatenate([np.arange(h * T_OWN, (h + 1) * T_OWN),
                           np.arange((1 - h) * T_OWN, (2 - h) * T_OWN)]).astype(np.float32)
    qpos = kpos[:T_OWN]
    wqr = np.asarray(inputs["L0_attn_Wqr"])
    wqrp = np.zeros((D, D), np.float32)
    for hh in range(16):
        wqrp[:, hh * 64 + 0: hh * 64 + 16] = wqr[:, hh * 32: hh * 32 + 16]
        wqrp[:, hh * 64 + 32: hh * 64 + 48] = wqr[:, hh * 32 + 16: hh * 32 + 32]
    wkr = np.asarray(inputs["L0_attn_Wkr"])
    wkrp = np.zeros((D, 64), np.float32)
    wkrp[:, 0:16] = wkr[:, 0:16]
    wkrp[:, 32:48] = wkr[:, 16:32]
    tcq, tsq = _rope_l0_tables(qpos, 1.0)
    tck, tsk = _rope_l0_tables(kpos, 1.0 / np.sqrt(np.float32(96.0)))
    cmask = (kpos[:, None] <= qpos[None, :]).astype(BFNP)
    return {
        "xin": f32c(xperm),
        "wq": f32c(np.asarray(inputs["L0_attn_Wq"])),
        "wqrp": f32c(wqrp),
        "wkrp": f32c(wkrp),
        "wkvd": f32c(np.asarray(inputs["L0_attn_Wkvd"])),
        "wku": f32c(np.asarray(inputs["L0_attn_Wku"])),
        "wvu": f32c(np.asarray(inputs["L0_attn_Wvu"])),
        "wo": f32c(np.asarray(inputs["L0_attn_Wo"])),
        "lobo": f32c(np.asarray(inputs["L0_attn_lobo_log"]).reshape(1, 16)),
        "tcq": np.ascontiguousarray(tcq),
        "tsq": np.ascontiguousarray(tsq),
        "tck": np.ascontiguousarray(tck),
        "tsk": np.ascontiguousarray(tsk),
        "cmask": np.ascontiguousarray(cmask),
    }



PARTS = [
    ("L0a", build_L0a, stage_L0a_inputs),
    ("L0f", build_L0f, stage_L0f_inputs),
    ("L1a", build_L1a, stage_L1a_inputs),
    ("L1f", build_L1f, stage_L1f_inputs),
    ("L2a", build_L2a, stage_L2a_inputs),
    ("L2f", build_L2f, stage_L2f_inputs),
    ("L3a", build_L3a, stage_L3a_inputs),
    ("L3f", build_L3f, stage_L3f_inputs),
]


def run_part(build, stage, inputs):
    P = build()
    try:
        in_maps = []
        for c in range(N_CORES):
            m = dict(P.host_const)
            m.update(stage(inputs, c))
            in_maps.append(m)
        res = run_bass_kernel_spmd(P.nc, in_maps, core_ids=list(range(N_CORES)))
    finally:
        P.es.close()
    x_new = np.empty((4, T_FULL, D), np.float32)
    for c in range(N_CORES):
        b, h = c // 2, c % 2
        x_new[b, h * T_OWN:(h + 1) * T_OWN, :] = res.results[c]["xout"]
    return x_new


def kernel(**inputs):
    x = np.asarray(inputs["L0_attn_x"], np.float32)
    inputs = dict(inputs)
    inputs["_x_cur"] = x
    for name, build, stage in PARTS:
        inputs["_x_cur"] = run_part(build, stage, inputs)
    return inputs["_x_cur"]
